# version 03
# baseline (speedup 1.0000x reference)
import numpy as np
import concourse.bass as bass
import concourse.mybir as mybir
from concourse.bass_utils import run_bass_kernel_spmd

F32 = mybir.dt.float32
F32R = mybir.dt.float32r
BF16 = mybir.dt.bfloat16
ALU = mybir.AluOpType
ACT = mybir.ActivationFunctionType
AX = mybir.AxisListType.X

D, DI, DBC, DH = 1024, 2048, 4096, 2816
KD, KI, KB, KH = D // 128, DI // 128, DBC // 128, DH // 128
T = 512
NSEQ, SEQ = 2, 2048
NT_SEQ = SEQ // T
NT = NSEQ * NT_SEQ
SECT_H = 11
NSEC_H = KH // SECT_H
SECT_Y = 8
NSEC_Y = KI // SECT_Y

ENGS = ("sync", "vector", "scalar", "gpsimd", "tensor")
SELF = {"vector": "vS", "scalar": "sS", "gpsimd": "gS"}


class Buf:
    __slots__ = ("w", "rs")

    def __init__(self):
        self.w = {}
        self.rs = {}

    @staticmethod
    def _merge(d, tag):
        if tag is None:
            return
        k, v = tag
        if d.get(k, 0) < v:
            d[k] = v


class Plan:
    def __init__(self, nc):
        self.nc = nc
        self.ops = {e: [] for e in ENGS}
        self.cnt = {"vS": 0, "sS": 0, "gS": 0, "tS": 0, "qa": 0, "qw1": 0, "qw2": 0}
        self.seen = {e: {} for e in ENGS}
        self.semkeys = ["vS", "sS", "gS", "tS", "qa", "qw1", "qw2"]
        self.sems = {}

    def _waits(self, eng, bufs_r, bufs_w, extra=()):
        tags = {}
        for b in bufs_r:
            for k, v in b.w.items():
                if tags.get(k, 0) < v:
                    tags[k] = v
        for b in bufs_w:
            for d in (b.w, b.rs):
                for k, v in d.items():
                    if tags.get(k, 0) < v:
                        tags[k] = v
        for k, v in extra:
            if tags.get(k, 0) < v:
                tags[k] = v
        out = []
        seen = self.seen[eng]
        skip = SELF.get(eng)
        for k, v in tags.items():
            if k == skip:
                continue
            if seen.get(k, 0) >= v:
                continue
            seen[k] = v
            out.append((k, v))
        return out

    def _post(self, tag, bufs_r, bufs_w):
        for b in bufs_r:
            Buf._merge(b.rs, tag)
        for b in bufs_w:
            b.w = {}
            b.rs = {}
            Buf._merge(b.w, tag)

    def op(self, eng, fn, r=(), w=(), extra=()):
        sk = SELF[eng]
        waits = self._waits(eng, r, w, extra)
        n = self.cnt[sk]
        self.cnt[sk] = n + 1

        def emit(e, fn=fn, waits=waits, n=n, sk=sk):
            for k, v in waits:
                e.wait_ge(self.sems[k], v)
            ins = fn(e)
            if n > 0:
                ins._wait_ge(self.sems[sk], n)
            ins.then_inc(self.sems[sk], 1)

        self.ops[eng].append(emit)
        tag = (sk, n + 1)
        self._post(tag, r, w)
        return tag

    def tgroup(self, fns, r=(), w=()):
        waits = self._waits("tensor", r, w)
        n = self.cnt["tS"] + 1
        self.cnt["tS"] = n

        def emit(e, fns=fns, waits=waits):
            for k, v in waits:
                e.wait_ge(self.sems[k], v)
            ins = None
            for fn in fns:
                ins = fn(e)
            ins.then_inc(self.sems["tS"], 1)

        self.ops["tensor"].append(emit)
        tag = ("tS", n)
        self._post(tag, r, w)
        return tag

    def dma(self, q, fn, r=(), w=()):
        waits = self._waits("sync", r, w)
        self.cnt[q] += 16
        val = self.cnt[q]

        def emit(e, fn=fn, waits=waits, q=q):
            for k, v in waits:
                e.wait_ge(self.sems[k], v)
            fn(e).then_inc(self.sems[q], 16)

        self.ops["sync"].append(emit)
        tag = (q, val)
        self._post(tag, r, w)
        return tag

    def sync_wait(self, tags):
        waits = self._waits("sync", (), (), extra=tags)

        def emit(e, waits=waits):
            for k, v in waits:
                e.wait_ge(self.sems[k], v)

        self.ops["sync"].append(emit)


def _build_nc():
    nc = bass.Bass("TRN2", target_bir_lowering=False)
    P = Plan(nc)

    xin = nc.dram_tensor("x", [NSEQ, D, SEQ], F32, kind="ExternalInput").ap()
    yout = nc.dram_tensor("y", [NSEQ, D, SEQ], F32, kind="ExternalOutput").ap()
    wg = nc.dram_tensor("wg", [DH, D], F32, kind="ExternalInput").ap()
    wu = nc.dram_tensor("wu", [DH, D], F32, kind="ExternalInput").ap()
    wd = nc.dram_tensor("wd", [D, DH], F32, kind="ExternalInput").ap()
    wi = nc.dram_tensor("wi", [D, DBC], F32, kind="ExternalInput").ap()
    wb = nc.dram_tensor("wb", [DI, DBC], F32, kind="ExternalInput").ap()
    wo = nc.dram_tensor("wo", [DI, D], F32, kind="ExternalInput").ap()
    bin_ = nc.dram_tensor("bin", [128, KB], F32, kind="ExternalInput").ap()
    bbc = nc.dram_tensor("bbc", [128, KB], F32, kind="ExternalInput").ap()
    bout = nc.dram_tensor("bout", [128, KD], F32, kind="ExternalInput").ap()
    cwd = nc.dram_tensor("cwd", [128, 4, KI], F32, kind="ExternalInput").ap()
    cbd = nc.dram_tensor("cbd", [128, KI], F32, kind="ExternalInput").ap()
    ldd = nc.dram_tensor("ldd", [128, KI], F32, kind="ExternalInput").ap()
    ident_d = nc.inline_tensor(np.eye(128, dtype=np.float32), name="ident").ap()

    SG = [nc.dram_tensor(f"SG{i}", [KH, 128, KD * 128], F32R, kind="Internal").ap() for i in range(2)]
    SU = [nc.dram_tensor(f"SU{i}", [KH, 128, KD * 128], F32R, kind="Internal").ap() for i in range(2)]
    SD = [nc.dram_tensor(f"SD{i}", [KD, 128, KH * 128], F32R, kind="Internal").ap() for i in range(2)]
    SI = [nc.dram_tensor(f"SI{i}", [KB, 128, KD * 128], F32R, kind="Internal").ap() for i in range(2)]
    SB = [nc.dram_tensor(f"SB{i}", [KB, 128, KI * 128], F32R, kind="Internal").ap() for i in range(2)]
    SO = [nc.dram_tensor(f"SO{i}", [KD, 128, KI * 128], F32R, kind="Internal").ap() for i in range(2)]

    A = nc.alloc_sbuf_tensor
    X = A("X", [128, KD * T], F32)
    AQ1 = A("AQ1", [128, KD * T], F32R)
    AQ2 = A("AQ2", [128, KD * T], F32R)
    BIG1 = A("BIG1", [128, KI * T], F32R)
    BIG2 = A("BIG2", [128, KI * T], F32R)
    SIGG = A("SIGG", [128, KI * T], BF16)
    CH1 = A("CH1", [128, T], F32)
    QT = A("QT", [128, T], F32)
    QAB = A("QAB", [128, T], F32)
    QSG = A("QSG", [128, T], F32)
    QC = A("QC", [128, T], F32)
    QD = A("QD", [128, T], F32)
    UW = A("UW", [128, T + 4], F32)
    QBM = A("QBM", [128, 176], F32)
    QSC = A("QSC", [128, 176], F32)
    QIV = A("QIV", [128, 176], F32)
    SMU = A("SMU", [1, T], F32)
    SVR = A("SVR", [1, T], F32)
    ST1 = A("ST1", [1, T], F32)
    SY0 = A("SY0", [1, T], F32)
    SIV = A("SIV", [1, T], F32)
    MUr = A("MUr", [1, T], F32R)
    IVr = A("IVr", [1, T], F32R)
    XRr = A("XRr", [128, T], F32R)
    SQF = A("SQF", [128, T], F32)
    SQR = A("SQR", [128, T], F32R)
    UH = A("UH", [128, KI * 3], F32)
    SCAR = A("SCAR", [128, KI], F32)
    ID = A("ID", [128, 128], F32)
    ONESC = A("ONESC", [128, 1], F32R)
    ONESCf = A("ONESCf", [128, 1], F32)
    ONESR = A("ONESR", [1, 128], F32R)
    ONESRf = A("ONESRf", [1, 128], F32)
    BIN = A("BIN", [128, KB], F32)
    BBC = A("BBC", [128, KB], F32)
    BOUT = A("BOUT", [128, KD], F32)
    CW = A("CW", [128, 4 * KI], F32)
    CB = A("CB", [128, KI], F32)
    LDT = A("LDT", [128, KI], F32)
    AD = A("AD", [128, KI], F32)
    T16A = A("T16A", [128, KI], F32)
    T16B = A("T16B", [128, KI], F32)
    POOL = [A(f"POOL{i}", [128, KI * 128], F32R) for i in range(4)]

    PB = [nc.alloc_psum_tensor(f"pb{i}", [128, T], F32) for i in range(8)]

    bufs = {}

    def B(name):
        if name not in bufs:
            bufs[name] = Buf()
        return bufs[name]

    bX, bAQ1, bAQ2, bBIG1, bBIG2, bSIGG = B("X"), B("AQ1"), B("AQ2"), B("BIG1"), B("BIG2"), B("SIGG")
    bCH1, bQT, bQAB, bQSG, bQC, bQD = B("CH1"), B("QT"), B("QAB"), B("QSG"), B("QC"), B("QD")
    bUW, bQBM, bQSC, bQIV, bSM = B("UW"), B("QBM"), B("QSC"), B("QIV"), B("SM")
    bMUr, bIVr, bXRr, bSQF, bSQR = B("MUr"), B("IVr"), B("XRr"), B("SQF"), B("SQR")
    bUH, bSCAR = B("UH"), B("SCAR")
    bPOOL = [B(f"POOL{i}") for i in range(4)]
    bPB = [B(f"pb{i}") for i in range(8)]
    bSG = [B("SG0"), B("SG1")]
    bSU = [B("SU0"), B("SU1")]
    bSD = [B("SD0"), B("SD1")]
    bSI = [B("SI0"), B("SI1")]
    bSBc = [B("SBc0"), B("SBc1")]
    bSO = [B("SO0"), B("SO1")]
    bconsts = B("consts")

    P.dma("qa", lambda e: e.dma_start(out=ID[:], in_=ident_d), w=[bconsts])
    for src, dst in ((bin_, BIN), (bbc, BBC), (bout, BOUT), (cbd, CB), (ldd, LDT)):
        P.dma("qa", lambda e, s=src, d=dst: e.dma_start(out=d[:], in_=s), w=[bconsts])
    P.dma("qa", lambda e: e.dma_start(
        out=CW[:].rearrange("p (a j) -> p a j", a=4), in_=cwd), w=[bconsts])
    P.op("vector", lambda e: e.memset(ONESCf[:], 1.0), w=[bconsts])
    P.op("vector", lambda e: e.memset(ONESRf[:], 1.0), w=[bconsts])
    P.op("vector", lambda e: e.tensor_copy(ONESC[:], ONESCf[:]), r=[bconsts], w=[bconsts])
    P.op("vector", lambda e: e.tensor_copy(ONESR[:], ONESRf[:]), r=[bconsts], w=[bconsts])
    P.op("scalar", lambda e: e.activation(T16A[:], LDT[:], ACT.Abs, scale=-1.0), r=[bconsts], w=[B("T16A")])
    P.op("scalar", lambda e: e.activation(T16A[:], T16A[:], ACT.Exp), r=[B("T16A")], w=[B("T16A")])
    P.op("vector", lambda e: e.tensor_single_scalar(T16A[:], T16A[:], 1.0, ALU.add), r=[B("T16A")], w=[B("T16A")])
    P.op("scalar", lambda e: e.activation(T16A[:], T16A[:], ACT.Ln), r=[B("T16A")], w=[B("T16A")])
    P.op("vector", lambda e: e.tensor_single_scalar(T16B[:], LDT[:], 0.0, ALU.max), r=[bconsts], w=[B("T16B")])
    P.op("vector", lambda e: e.tensor_add(T16A[:], T16A[:], T16B[:]), r=[B("T16A"), B("T16B")], w=[B("T16A")])
    P.op("scalar", lambda e: e.activation(AD[:], T16A[:], ACT.Exp, scale=-1.0), r=[B("T16A")], w=[bconsts])

    def staircase(src_buf, nblk):
        v4 = QT[:].rearrange("p (g b k) -> p g b k", g=4, k=16)
        bm = QBM[:, 0:4 * nblk].rearrange("p (g b) -> p g b", g=4)
        P.op("vector", lambda e: e.tensor_reduce(bm, v4, AX, ALU.max, apply_absolute_value=True),
             r=[bQT], w=[bQBM])
        P.op("vector", lambda e: e.tensor_scalar(QSC[:, 0:4 * nblk], QBM[:, 0:4 * nblk],
                                                 1.0 / 6.0, 1e-12, ALU.mult, ALU.max),
             r=[bQBM], w=[bQSC])
        P.op("vector", lambda e: e.reciprocal(QIV[:, 0:4 * nblk], QSC[:, 0:4 * nblk]),
             r=[bQSC], w=[bQIV])
        ivb = QIV[:, 0:4 * nblk].rearrange("p (g b) -> p g b", g=4).unsqueeze(3).broadcast_to([128, 4, nblk, 16])
        P.op("vector", lambda e: e.tensor_tensor(v4, v4, ivb, ALU.mult), r=[bQIV], w=[bQT])
        P.op("scalar", lambda e: e.activation(QAB[:], QT[:], ACT.Abs), r=[bQT], w=[bQAB])
        P.op("scalar", lambda e: e.activation(QSG[:], QT[:], ACT.Sign), r=[bQT], w=[bQSG])
        P.op("vector", lambda e: e.tensor_single_scalar(QC[:], QAB[:], 0.25, ALU.is_gt), r=[bQAB], w=[bQC])
        for th in (0.75, 1.25, 1.75):
            P.op("vector", lambda e, th=th: e.scalar_tensor_tensor(QC[:], QAB[:], th, QC[:], ALU.is_gt, ALU.add),
                 r=[bQAB], w=[bQC])
        P.op("vector", lambda e: e.tensor_single_scalar(QD[:], QAB[:], 2.5, ALU.is_gt), r=[bQAB], w=[bQD])
        P.op("vector", lambda e: e.scalar_tensor_tensor(QD[:], QAB[:], 3.5, QD[:], ALU.is_gt, ALU.add),
             r=[bQAB], w=[bQD])
        P.op("vector", lambda e: e.scalar_tensor_tensor(QC[:], QC[:], 0.5, QD[:], ALU.mult, ALU.add),
             r=[bQD], w=[bQC])
        P.op("vector", lambda e: e.tensor_scalar(QD[:], QAB[:], 5.0, 2.0, ALU.is_gt, ALU.mult),
             r=[bQAB], w=[bQD])
        P.op("vector", lambda e: e.tensor_add(QC[:], QC[:], QD[:]), r=[bQD], w=[bQC])
        P.op("vector", lambda e: e.tensor_mul(QC[:], QC[:], QSG[:]), r=[bQSG], w=[bQC])
        scb = QSC[:, 0:4 * nblk].rearrange("p (g b) -> p g b", g=4).unsqueeze(3).broadcast_to([128, 4, nblk, 16])
        qc4 = QC[:].rearrange("p (g b k) -> p g b k", g=4, k=16)
        P.op("vector", lambda e: e.tensor_tensor(v4, qc4, scb, ALU.mult), r=[bQC, bQSC], w=[bQT])

    def quant_tile(src_sb, src_buf, q1_ap, q2_ap, dst1_buf, dst2_buf, nblk, pbA, pbB):
        bA, bB = bPB[pbA], bPB[pbB]
        P.tgroup([(lambda e, g=g: e.transpose(PB[pbA][:, g * 128:(g + 1) * 128],
                                              src_sb[:, g * 128:(g + 1) * 128], ID[:]))
                  for g in range(4)], r=[src_buf, bconsts], w=[bA])
        P.op("scalar", lambda e: e.copy(QT[:], PB[pbA][:]), r=[bA], w=[bQT])
        staircase(src_buf, nblk)
        P.tgroup([(lambda e, g=g: e.transpose(PB[pbB][:, g * 128:(g + 1) * 128],
                                              QT[:, g * 128:(g + 1) * 128], ID[:]))
                  for g in range(4)], r=[bQT, bconsts], w=[bB])
        P.op("vector", lambda e: e.tensor_copy(q1_ap, PB[pbB][:]), r=[bB], w=[dst1_buf])
        P.op("vector", lambda e: e.tensor_tensor(q2_ap, PB[pbB][:], q1_ap.bitcast(F32), ALU.subtract),
             r=[bB, dst1_buf], w=[dst2_buf])

    def prep_ffn_weight(src, s1, s2, bs1, bs2, nrows_chunks, kfree):
        pt1 = AQ2[:, 0:512]
        pt2 = AQ2[:, 512:1024]
        for j in range(nrows_chunks):
            for c0 in range(0, kfree, 512):
                cw_ = min(512, kfree - c0)
                nb = cw_ // 16
                nk = cw_ // 128
                P.dma("qa", lambda e, j=j, c0=c0, cw_=cw_: e.dma_start(
                    out=QT[:, 0:cw_], in_=src[j * 128:(j + 1) * 128, c0:c0 + cw_]),
                    r=[bQT], w=[bQT])
                v16 = QT[:, 0:cw_].rearrange("p (b k) -> p b k", k=16)
                bm = QBM[:, 0:nb]
                P.op("vector", lambda e, v16=v16, bm=bm: e.tensor_reduce(
                    bm, v16, AX, ALU.max, apply_absolute_value=True), r=[bQT], w=[bQBM])
                P.op("vector", lambda e, bm=bm, nb=nb: e.tensor_scalar(
                    QSC[:, 0:nb], bm, 1.0 / 6.0, 1e-12, ALU.mult, ALU.max), r=[bQBM], w=[bQSC])
                P.op("vector", lambda e, nb=nb: e.reciprocal(QIV[:, 0:nb], QSC[:, 0:nb]),
                     r=[bQSC], w=[bQIV])
                ivb = QIV[:, 0:nb].unsqueeze(2).broadcast_to([128, nb, 16])
                scb = QSC[:, 0:nb].unsqueeze(2).broadcast_to([128, nb, 16])
                P.op("vector", lambda e, v16=v16, ivb=ivb: e.tensor_tensor(
                    v16, v16, ivb, ALU.mult), r=[bQIV], w=[bQT])
                P.op("scalar", lambda e, cw_=cw_: e.activation(QSG[:, 0:cw_], QT[:, 0:cw_], ACT.Sign),
                     r=[bQT], w=[bQSG])
                P.op("scalar", lambda e, cw_=cw_: e.activation(QAB[:, 0:cw_], QT[:, 0:cw_], ACT.Abs),
                     r=[bQT], w=[bQAB])
                P.op("vector", lambda e, cw_=cw_: e.tensor_single_scalar(
                    QC[:, 0:cw_], QAB[:, 0:cw_], 0.25, ALU.is_gt), r=[bQAB], w=[bQC])
                for th in (0.75, 1.25, 1.75):
                    P.op("vector", lambda e, th=th, cw_=cw_: e.scalar_tensor_tensor(
                        QC[:, 0:cw_], QAB[:, 0:cw_], th, QC[:, 0:cw_], ALU.is_gt, ALU.add),
                        r=[bQAB], w=[bQC])
                P.op("vector", lambda e, cw_=cw_: e.tensor_single_scalar(
                    QD[:, 0:cw_], QAB[:, 0:cw_], 2.5, ALU.is_gt), r=[bQAB], w=[bQD])
                P.op("vector", lambda e, cw_=cw_: e.scalar_tensor_tensor(
                    QD[:, 0:cw_], QAB[:, 0:cw_], 3.5, QD[:, 0:cw_], ALU.is_gt, ALU.add),
                    r=[bQAB], w=[bQD])
                P.op("vector", lambda e, cw_=cw_: e.scalar_tensor_tensor(
                    QC[:, 0:cw_], QC[:, 0:cw_], 0.5, QD[:, 0:cw_], ALU.mult, ALU.add),
                    r=[bQD], w=[bQC])
                P.op("vector", lambda e, cw_=cw_: e.tensor_scalar(
                    QD[:, 0:cw_], QAB[:, 0:cw_], 5.0, 2.0, ALU.is_gt, ALU.mult),
                    r=[bQAB], w=[bQD])
                P.op("vector", lambda e, cw_=cw_: e.tensor_add(
                    QC[:, 0:cw_], QC[:, 0:cw_], QD[:, 0:cw_]), r=[bQD], w=[bQC])
                P.op("vector", lambda e, cw_=cw_: e.tensor_mul(
                    QC[:, 0:cw_], QC[:, 0:cw_], QSG[:, 0:cw_]), r=[bQSG], w=[bQC])
                qc16 = QC[:, 0:cw_].rearrange("p (b k) -> p b k", k=16)
                P.op("vector", lambda e, qc16=qc16, scb=scb: e.tensor_tensor(
                    qc16, qc16, scb, ALU.mult), r=[bQSC], w=[bQC])
                P.tgroup([(lambda e, kk=kk: e.transpose(PB[0][:, kk * 128:(kk + 1) * 128],
                                                        QC[:, kk * 128:(kk + 1) * 128], ID[:]))
                          for kk in range(nk)], r=[bQC, bconsts], w=[bPB[0]])
                wdt = nk * 128
                P.op("vector", lambda e, wdt=wdt: e.tensor_copy(pt1[:, 0:wdt], PB[0][:, 0:wdt]),
                     r=[bPB[0]], w=[bAQ2])
                P.op("vector", lambda e, wdt=wdt: e.tensor_tensor(
                    pt2[:, 0:wdt], PB[0][:, 0:wdt], pt1[:, 0:wdt].bitcast(F32), ALU.subtract),
                    r=[bPB[0], bAQ2], w=[bAQ2])
                P.dma("qa", lambda e, j=j, c0=c0, wdt=wdt: e.dma_start(
                    out=s1[j, :, c0:c0 + wdt], in_=pt1[:, 0:wdt]), r=[bAQ2], w=[bs1])
                P.dma("qa", lambda e, j=j, c0=c0, wdt=wdt: e.dma_start(
                    out=s2[j, :, c0:c0 + wdt], in_=pt2[:, 0:wdt]), r=[bAQ2], w=[bs2])

    prep_ffn_weight(wg, SG[0], SG[1], bSG[0], bSG[1], KH, D)
    prep_ffn_weight(wu, SU[0], SU[1], bSU[0], bSU[1], KH, D)
    prep_ffn_weight(wd, SD[0], SD[1], bSD[0], bSD[1], KD, DH)

    def prep_mm_weight(src, s1, s2, bs1, bs2, kchunks, m):
        mb = m // 128
        w1 = AQ1[:, 0:512]
        w2 = AQ2[:, 0:512]
        for k in range(kchunks):
            for b0 in range(0, mb, 4):
                bw = min(4, mb - b0)
                cw_ = bw * 128
                P.dma("qa", lambda e, k=k, b0=b0, cw_=cw_: e.dma_start(
                    out=QT[:, 0:cw_], in_=src[k * 128:(k + 1) * 128, b0 * 128:b0 * 128 + cw_]),
                    r=[bQT], w=[bQT])
                P.op("vector", lambda e, cw_=cw_: e.tensor_copy(w1[:, 0:cw_], QT[:, 0:cw_]),
                     r=[bQT], w=[bAQ1])
                P.op("vector", lambda e, cw_=cw_: e.tensor_tensor(
                    w2[:, 0:cw_], QT[:, 0:cw_], w1[:, 0:cw_].bitcast(F32), ALU.subtract),
                    r=[bQT, bAQ1], w=[bAQ2])
                for ws, bws, sd, bd in ((w1, bAQ1, s1, bs1), (w2, bAQ2, s2, bs2)):
                    P.dma("qa", lambda e, ws=ws, sd=sd, k=k, b0=b0, bw=bw, cw_=cw_: e.dma_start(
                        out=sd[b0:b0 + bw, :, k * 128:(k + 1) * 128].transpose([1, 0, 2]),
                        in_=ws[:, 0:cw_].rearrange("p (b q) -> p b q", b=bw)),
                        r=[bws], w=[bd])

    prep_mm_weight(wi, SI[0], SI[1], bSI[0], bSI[1], KD, DBC)
    prep_mm_weight(wb, SB[0], SB[1], bSBc[0], bSBc[1], KI, DBC)
    prep_mm_weight(wo, SO[0], SO[1], bSO[0], bSO[1], KI, D)

    def layernorm_prep():
        for ch in range(KD):
            xc = X[:, ch * T:(ch + 1) * T]
            P.op("vector", lambda e, xc=xc: e.tensor_copy(XRr[:], xc), r=[bX], w=[bXRr])
            P.tgroup([(lambda e, ch=ch: e.matmul(PB[6][0:1, :], ONESC[:], XRr[:],
                                                 start=(ch == 0), stop=(ch == KD - 1),
                                                 skip_group_check=True))],
                     r=[bXRr, bconsts], w=[bPB[6]])
            P.op("vector", lambda e, xc=xc: e.tensor_tensor(SQF[:], xc, xc, ALU.mult), r=[bX], w=[bSQF])
            P.op("vector", lambda e: e.tensor_copy(SQR[:], SQF[:]), r=[bSQF], w=[bSQR])
            P.tgroup([(lambda e, ch=ch: e.matmul(PB[7][0:1, :], ONESC[:], SQR[:],
                                                 start=(ch == 0), stop=(ch == KD - 1),
                                                 skip_group_check=True))],
                     r=[bSQR, bconsts], w=[bPB[7]])
        mu, var, t1, y0, ivs = SMU[:], SVR[:], ST1[:], SY0[:], SIV[:]
        P.op("vector", lambda e: e.tensor_single_scalar(mu, PB[6][0:1, :], 1.0 / D, ALU.mult),
             r=[bPB[6]], w=[bSM])
        P.op("vector", lambda e: e.tensor_single_scalar(var, PB[7][0:1, :], 1.0 / D, ALU.mult),
             r=[bPB[7]], w=[bSM])
        P.op("vector", lambda e: e.tensor_mul(t1, mu, mu), r=[bSM], w=[bSM])
        P.op("vector", lambda e: e.tensor_sub(var, var, t1), r=[bSM], w=[bSM])
        P.op("vector", lambda e: e.tensor_single_scalar(var, var, 1e-5, ALU.add), r=[bSM], w=[bSM])
        P.op("vector", lambda e: e.reciprocal(t1, var), r=[bSM], w=[bSM])
        P.op("scalar", lambda e: e.activation(y0, t1, ACT.Sqrt), r=[bSM], w=[bSM])
        P.op("vector", lambda e: e.tensor_mul(ivs, y0, y0), r=[bSM], w=[bSM])
        P.op("vector", lambda e: e.tensor_mul(ivs, ivs, var), r=[bSM], w=[bSM])
        P.op("vector", lambda e: e.tensor_scalar(ivs, ivs, -0.5, 1.5, ALU.mult, ALU.add), r=[bSM], w=[bSM])
        P.op("vector", lambda e: e.tensor_mul(ivs, ivs, y0), r=[bSM], w=[bSM])
        P.op("vector", lambda e: e.tensor_copy(MUr[:], mu), r=[bSM], w=[bMUr])
        P.op("vector", lambda e: e.tensor_copy(IVr[:], ivs), r=[bSM], w=[bIVr])
        P.tgroup([(lambda e: e.matmul(PB[4][:], ONESR[:], MUr[:], start=True, stop=True))],
                 r=[bMUr, bconsts], w=[bPB[4]])
        P.tgroup([(lambda e: e.matmul(PB[5][:], ONESR[:], IVr[:], start=True, stop=True))],
                 r=[bIVr, bconsts], w=[bPB[5]])

    def ln_chunk(ch):
        xc = X[:, ch * T:(ch + 1) * T]
        P.op("vector", lambda e, xc=xc: e.scalar_tensor_tensor(CH1[:], xc, 1.0, PB[4][:],
                                                               ALU.bypass, ALU.subtract),
             r=[bX, bPB[4]], w=[bCH1])
        P.op("vector", lambda e: e.tensor_tensor(CH1[:], CH1[:], PB[5][:], ALU.mult),
             r=[bPB[5]], w=[bCH1])

    qwkey_f = "qw1"

    pool_i = [0]

    def next_slot():
        i = pool_i[0] % 4
        pool_i[0] += 1
        return i

    def ffn_sublayer():
        layernorm_prep()
        for ch in range(KD):
            ln_chunk(ch)
            quant_tile(CH1, bCH1, AQ1[:, ch * T:(ch + 1) * T], AQ2[:, ch * T:(ch + 1) * T],
                       bAQ1, bAQ2, KD, 0, 1)
        for s in range(NSEC_H):
            ms = list(range(s * SECT_H, (s + 1) * SECT_H))
            for ci, mb in enumerate(ms):
                sl_a, sl_b = next_slot(), next_slot()
                pg1, pu1 = POOL[sl_a][:, 0:KD * 128], POOL[sl_a][:, KD * 128:2 * KD * 128]
                pg2, pu2 = POOL[sl_b][:, 0:KD * 128], POOL[sl_b][:, KD * 128:2 * KD * 128]
                P.dma(qwkey_f, lambda e, mb=mb, d=pg1: e.dma_start(out=d, in_=SG[0][mb]), r=[bSG[0]], w=[bPOOL[sl_a]])
                P.dma(qwkey_f, lambda e, mb=mb, d=pu1: e.dma_start(out=d, in_=SU[0][mb]), r=[bSU[0]], w=[bPOOL[sl_a]])
                P.dma(qwkey_f, lambda e, mb=mb, d=pg2: e.dma_start(out=d, in_=SG[1][mb]), r=[bSG[1]], w=[bPOOL[sl_b]])
                P.dma(qwkey_f, lambda e, mb=mb, d=pu2: e.dma_start(out=d, in_=SU[1][mb]), r=[bSU[1]], w=[bPOOL[sl_b]])
                for (p1, p2, pbi) in ((pg1, pg2, 2), (pu1, pu2, 3)):
                    fns = []
                    for k in range(KD):
                        st1 = p1[:, k * 128:(k + 1) * 128]
                        st2 = p2[:, k * 128:(k + 1) * 128]
                        m1 = AQ1[:, k * T:(k + 1) * T]
                        m2 = AQ2[:, k * T:(k + 1) * T]
                        fns.append(lambda e, st1=st1, m1=m1, k=k, pbi=pbi: e.matmul(
                            PB[pbi][:], st1, m1, start=(k == 0), stop=False))
                        fns.append(lambda e, st1=st1, m2=m2, pbi=pbi: e.matmul(
                            PB[pbi][:], st1, m2, start=False, stop=False))
                        fns.append(lambda e, st2=st2, m1=m1, k=k, pbi=pbi: e.matmul(
                            PB[pbi][:], st2, m1, start=False, stop=(k == KD - 1)))
                    P.tgroup(fns, r=[bAQ1, bAQ2, bPOOL[sl_a], bPOOL[sl_b]], w=[bPB[pbi]])
                P.op("scalar", lambda e: e.activation(CH1[:], PB[2][:], ACT.Silu), r=[bPB[2]], w=[bCH1])
                P.op("vector", lambda e: e.tensor_tensor(CH1[:], CH1[:], PB[3][:], ALU.mult),
                     r=[bPB[3]], w=[bCH1])
                quant_tile(CH1, bCH1, BIG1[:, ci * T:(ci + 1) * T], BIG2[:, ci * T:(ci + 1) * T],
                           bBIG1, bBIG2, KD, 0, 1)
            for mo in range(KD):
                sl = next_slot()
                sl2 = next_slot()
                pd1 = POOL[sl][:, 0:SECT_H * 128]
                pd2 = POOL[sl2][:, 0:SECT_H * 128]
                c0 = s * SECT_H * 128
                P.dma(qwkey_f, lambda e, mo=mo, c0=c0, d=pd1: e.dma_start(
                    out=d, in_=SD[0][mo, :, c0:c0 + SECT_H * 128]), r=[bSD[0]], w=[bPOOL[sl]])
                P.dma(qwkey_f, lambda e, mo=mo, c0=c0, d=pd2: e.dma_start(
                    out=d, in_=SD[1][mo, :, c0:c0 + SECT_H * 128]), r=[bSD[1]], w=[bPOOL[sl2]])
                pbi = 2 + (mo % 2)
                fns = []
                for ci in range(SECT_H):
                    st1 = pd1[:, ci * 128:(ci + 1) * 128]
                    st2 = pd2[:, ci * 128:(ci + 1) * 128]
                    m1 = BIG1[:, ci * T:(ci + 1) * T]
                    m2 = BIG2[:, ci * T:(ci + 1) * T]
                    fns.append(lambda e, st1=st1, m1=m1, ci=ci, pbi=pbi: e.matmul(
                        PB[pbi][:], st1, m1, start=(ci == 0), stop=False))
                    fns.append(lambda e, st1=st1, m2=m2, pbi=pbi: e.matmul(
                        PB[pbi][:], st1, m2, start=False, stop=False))
                    fns.append(lambda e, st2=st2, m1=m1, ci=ci, pbi=pbi: e.matmul(
                        PB[pbi][:], st2, m1, start=False, stop=(ci == SECT_H - 1)))
                P.tgroup(fns, r=[bBIG1, bBIG2, bPOOL[sl], bPOOL[sl2]], w=[bPB[pbi]])
                xc = X[:, mo * T:(mo + 1) * T]
                P.op("vector", lambda e, xc=xc, pbi=pbi: e.scalar_tensor_tensor(
                    xc, PB[pbi][:], 0.5, xc, ALU.mult, ALU.add), r=[bPB[pbi]], w=[bX])

    def mamba_sublayer(c_pos):
        layernorm_prep()
        for ch in range(KD):
            ln_chunk(ch)
            P.op("vector", lambda e, ch=ch: e.tensor_copy(AQ1[:, ch * T:(ch + 1) * T], CH1[:]),
                 r=[bCH1], w=[bAQ1])
            P.op("vector", lambda e, ch=ch: e.tensor_tensor(AQ2[:, ch * T:(ch + 1) * T], CH1[:],
                                                            AQ1[:, ch * T:(ch + 1) * T].bitcast(F32),
                                                            ALU.subtract),
                 r=[bCH1, bAQ1], w=[bAQ2])
        for j in range(KI):
            sl_a, sl_b = next_slot(), next_slot()
            piu1, piu2 = POOL[sl_a][:, 0:KD * 128], POOL[sl_a][:, KD * 128:2 * KD * 128]
            pig1, pig2 = POOL[sl_b][:, 0:KD * 128], POOL[sl_b][:, KD * 128:2 * KD * 128]
            P.dma("qw2", lambda e, j=j, d=piu1: e.dma_start(out=d, in_=SI[0][j]), r=[bSI[0]], w=[bPOOL[sl_a]])
            P.dma("qw2", lambda e, j=j, d=piu2: e.dma_start(out=d, in_=SI[1][j]), r=[bSI[1]], w=[bPOOL[sl_a]])
            P.dma("qw2", lambda e, j=j, d=pig1: e.dma_start(out=d, in_=SI[0][KI + j]), r=[bSI[0]], w=[bPOOL[sl_b]])
            P.dma("qw2", lambda e, j=j, d=pig2: e.dma_start(out=d, in_=SI[1][KI + j]), r=[bSI[1]], w=[bPOOL[sl_b]])
            for (p1, p2, pbi) in ((piu1, piu2, 2), (pig1, pig2, 3)):
                fns = []
                for k in range(KD):
                    st1 = p1[:, k * 128:(k + 1) * 128]
                    st2 = p2[:, k * 128:(k + 1) * 128]
                    m1 = AQ1[:, k * T:(k + 1) * T]
                    m2 = AQ2[:, k * T:(k + 1) * T]
                    fns.append(lambda e, st1=st1, m1=m1, k=k, pbi=pbi: e.matmul(
                        PB[pbi][:], st1, m1, start=(k == 0), stop=False))
                    fns.append(lambda e, st1=st1, m2=m2, pbi=pbi: e.matmul(
                        PB[pbi][:], st1, m2, start=False, stop=False))
                    fns.append(lambda e, st2=st2, m1=m1, k=k, pbi=pbi: e.matmul(
                        PB[pbi][:], st2, m1, start=False, stop=(k == KD - 1)))
                P.tgroup(fns, r=[bAQ1, bAQ2, bPOOL[sl_a], bPOOL[sl_b]], w=[bPB[pbi]])
            if c_pos == 0:
                P.op("vector", lambda e: e.memset(UW[:, 0:3], 0.0), w=[bUW])
            else:
                P.op("vector", lambda e, j=j: e.tensor_copy(UW[:, 0:3], UH[:, j * 3:j * 3 + 3]),
                     r=[bUH], w=[bUW])
            P.op("vector", lambda e, j=j: e.tensor_scalar(UW[:, 3:3 + T], PB[2][:], BIN[:, j:j + 1],
                                                          None, ALU.add),
                 r=[bPB[2], bconsts], w=[bUW])
            P.op("vector", lambda e, j=j: e.tensor_copy(UH[:, j * 3:j * 3 + 3], UW[:, T:T + 3]),
                 r=[bUW], w=[bUH])
            P.op("vector", lambda e, j=j: e.tensor_scalar(QC[:], UW[:, 0:T], CW[:, 0 * KI + j:0 * KI + j + 1],
                                                          None, ALU.mult),
                 r=[bUW, bconsts], w=[bQC])
            P.op("vector", lambda e, j=j: e.scalar_tensor_tensor(QD[:], UW[:, 1:1 + T],
                                                                 CW[:, 1 * KI + j:1 * KI + j + 1],
                                                                 QC[:], ALU.mult, ALU.add),
                 r=[bUW, bQC, bconsts], w=[bQD])
            P.op("vector", lambda e, j=j: e.scalar_tensor_tensor(QC[:], UW[:, 2:2 + T],
                                                                 CW[:, 2 * KI + j:2 * KI + j + 1],
                                                                 QD[:], ALU.mult, ALU.add),
                 r=[bUW, bQD, bconsts], w=[bQC])
            P.op("vector", lambda e, j=j: e.scalar_tensor_tensor(QD[:], UW[:, 3:3 + T],
                                                                 CW[:, 3 * KI + j:3 * KI + j + 1],
                                                                 QC[:], ALU.mult, ALU.add),
                 r=[bUW, bQC, bconsts], w=[bQD])
            P.op("scalar", lambda e, j=j: e.activation(QT[:], QD[:], ACT.Silu, bias=CB[:, j:j + 1]),
                 r=[bQD, bconsts], w=[bQT])
            P.op("vector", lambda e, j=j: e.tensor_copy(BIG1[:, j * T:(j + 1) * T], QT[:]),
                 r=[bQT], w=[bBIG1])
            P.op("vector", lambda e, j=j: e.tensor_tensor(BIG2[:, j * T:(j + 1) * T], QT[:],
                                                          BIG1[:, j * T:(j + 1) * T].bitcast(F32),
                                                          ALU.subtract),
                 r=[bQT, bBIG1], w=[bBIG2])
            P.op("scalar", lambda e, j=j: e.activation(SIGG[:, j * T:(j + 1) * T], PB[3][:],
                                                       ACT.Sigmoid, bias=BIN[:, KI + j:KI + j + 1]),
                 r=[bPB[3], bconsts], w=[bSIGG])

        for j in range(KI):
            sl_a, sl_b = next_slot(), next_slot()
            pb1s, pb2s = POOL[sl_a][:], POOL[sl_b][:]
            P.dma("qw2", lambda e, j=j, d=pb1s: e.dma_start(out=d, in_=SB[0][j]), r=[bSBc[0]], w=[bPOOL[sl_a]])
            P.dma("qw2", lambda e, j=j, d=pb2s: e.dma_start(out=d, in_=SB[1][j]), r=[bSBc[1]], w=[bPOOL[sl_b]])
            fns = []
            for k in range(KI):
                st1 = pb1s[:, k * 128:(k + 1) * 128]
                st2 = pb2s[:, k * 128:(k + 1) * 128]
                m1 = BIG1[:, k * T:(k + 1) * T]
                m2 = BIG2[:, k * T:(k + 1) * T]
                fns.append(lambda e, st1=st1, m1=m1, k=k: e.matmul(
                    PB[2][:], st1, m1, start=(k == 0), stop=False))
                fns.append(lambda e, st1=st1, m2=m2: e.matmul(
                    PB[2][:], st1, m2, start=False, stop=False))
                fns.append(lambda e, st2=st2, m1=m1, k=k: e.matmul(
                    PB[2][:], st2, m1, start=False, stop=(k == KI - 1)))
            P.tgroup(fns, r=[bBIG1, bBIG2, bPOOL[sl_a], bPOOL[sl_b]], w=[bPB[2]])
            sl_c, sl_d = next_slot(), next_slot()
            pc1s, pc2s = POOL[sl_c][:], POOL[sl_d][:]
            P.dma("qw2", lambda e, j=j, d=pc1s: e.dma_start(out=d, in_=SB[0][KI + j]), r=[bSBc[0]], w=[bPOOL[sl_c]])
            P.dma("qw2", lambda e, j=j, d=pc2s: e.dma_start(out=d, in_=SB[1][KI + j]), r=[bSBc[1]], w=[bPOOL[sl_d]])
            fns = []
            for k in range(KI):
                st1 = pc1s[:, k * 128:(k + 1) * 128]
                st2 = pc2s[:, k * 128:(k + 1) * 128]
                m1 = BIG1[:, k * T:(k + 1) * T]
                m2 = BIG2[:, k * T:(k + 1) * T]
                fns.append(lambda e, st1=st1, m1=m1, k=k: e.matmul(
                    PB[3][:], st1, m1, start=(k == 0), stop=False))
                fns.append(lambda e, st1=st1, m2=m2: e.matmul(
                    PB[3][:], st1, m2, start=False, stop=False))
                fns.append(lambda e, st2=st2, m1=m1, k=k: e.matmul(
                    PB[3][:], st2, m1, start=False, stop=(k == KI - 1)))
            P.tgroup(fns, r=[bBIG1, bBIG2, bPOOL[sl_c], bPOOL[sl_d]], w=[bPB[3]])
            P.op("vector", lambda e, j=j: e.tensor_scalar(QT[:], PB[2][:], BBC[:, j:j + 1], None, ALU.add),
                 r=[bPB[2], bconsts], w=[bQT])
            P.op("vector", lambda e, j=j: e.tensor_scalar(QAB[:], PB[3][:], BBC[:, KI + j:KI + j + 1],
                                                          None, ALU.add),
                 r=[bPB[3], bconsts], w=[bQAB])
            a_b = AD[:, j:j + 1].broadcast_to([128, T])
            init = 0.0 if (c_pos == 0) else SCAR[:, j:j + 1]
            extra_r = [bconsts] if c_pos == 0 else [bconsts, bSCAR]
            P.op("vector", lambda e, a_b=a_b, init=init: e.tensor_tensor_scan(
                QSG[:], a_b, QT[:], init, ALU.mult, ALU.add),
                 r=[bQT] + extra_r, w=[bQSG])
            P.op("vector", lambda e, j=j: e.tensor_copy(SCAR[:, j:j + 1], QSG[:, T - 1:T]),
                 r=[bQSG], w=[bSCAR])
            P.op("vector", lambda e: e.tensor_mul(QT[:], QSG[:], QAB[:]), r=[bQSG, bQAB], w=[bQT])
            P.op("vector", lambda e, j=j: e.tensor_tensor(QT[:], QT[:], SIGG[:, j * T:(j + 1) * T],
                                                          ALU.mult),
                 r=[bSIGG], w=[bQT])
            jq = j % SECT_Y
            P.op("vector", lambda e, jq=jq: e.tensor_copy(AQ1[:, jq * T:(jq + 1) * T], QT[:]),
                 r=[bQT], w=[bAQ1])
            P.op("vector", lambda e, jq=jq: e.tensor_tensor(AQ2[:, jq * T:(jq + 1) * T], QT[:],
                                                            AQ1[:, jq * T:(jq + 1) * T].bitcast(F32),
                                                            ALU.subtract),
                 r=[bQT, bAQ1], w=[bAQ2])
            if jq == SECT_Y - 1:
                sec = j // SECT_Y
                for mo in range(KD):
                    sl = next_slot()
                    po1 = POOL[sl][:, 0:SECT_Y * 128]
                    po2 = POOL[sl][:, SECT_Y * 128:2 * SECT_Y * 128]
                    c0 = sec * SECT_Y * 128
                    P.dma("qw2", lambda e, mo=mo, c0=c0, d=po1: e.dma_start(
                        out=d, in_=SO[0][mo, :, c0:c0 + SECT_Y * 128]), r=[bSO[0]], w=[bPOOL[sl]])
                    P.dma("qw2", lambda e, mo=mo, c0=c0, d=po2: e.dma_start(
                        out=d, in_=SO[1][mo, :, c0:c0 + SECT_Y * 128]), r=[bSO[1]], w=[bPOOL[sl]])
                    pbi = 4 + (mo % 2)
                    fns = []
                    for jj in range(SECT_Y):
                        st1 = po1[:, jj * 128:(jj + 1) * 128]
                        st2 = po2[:, jj * 128:(jj + 1) * 128]
                        m1 = AQ1[:, jj * T:(jj + 1) * T]
                        m2 = AQ2[:, jj * T:(jj + 1) * T]
                        fns.append(lambda e, st1=st1, m1=m1, jj=jj, pbi=pbi: e.matmul(
                            PB[pbi][:], st1, m1, start=(jj == 0), stop=False))
                        fns.append(lambda e, st1=st1, m2=m2, pbi=pbi: e.matmul(
                            PB[pbi][:], st1, m2, start=False, stop=False))
                        fns.append(lambda e, st2=st2, m1=m1, jj=jj, pbi=pbi: e.matmul(
                            PB[pbi][:], st2, m1, start=False, stop=(jj == SECT_Y - 1)))
                    P.tgroup(fns, r=[bAQ1, bAQ2, bPOOL[sl]], w=[bPB[pbi]])
                    xc = X[:, mo * T:(mo + 1) * T]
                    if sec == NSEC_Y - 1:
                        P.op("vector", lambda e, mo=mo, pbi=pbi: e.tensor_scalar(
                            QC[:], PB[pbi][:], BOUT[:, mo:mo + 1], None, ALU.add),
                             r=[bPB[pbi], bconsts], w=[bQC])
                        P.op("vector", lambda e, xc=xc: e.tensor_add(xc, xc, QC[:]), r=[bQC], w=[bX])
                    else:
                        P.op("vector", lambda e, xc=xc, pbi=pbi: e.scalar_tensor_tensor(
                            xc, PB[pbi][:], 1.0, xc, ALU.bypass, ALU.add), r=[bPB[pbi]], w=[bX])

    out_tags = []
    for t in range(NT):
        s, c_pos = t // NT_SEQ, t % NT_SEQ
        P.dma("qa", lambda e, s=s, c_pos=c_pos: e.dma_start(
            out=X[:].rearrange("p (ch q) -> p ch q", ch=KD),
            in_=xin[s, :, c_pos * T:(c_pos + 1) * T].rearrange("(ch p) q -> p ch q", p=128)),
            r=[bX], w=[bX])
        ffn_sublayer()
        mamba_sublayer(c_pos)
        ffn_sublayer()
        tg = P.dma("qa", lambda e, s=s, c_pos=c_pos: e.dma_start(
            out=yout[s, :, c_pos * T:(c_pos + 1) * T].rearrange("(ch p) q -> p ch q", p=128),
            in_=X[:].rearrange("p (ch q) -> p ch q", ch=KD)),
            r=[bX], w=[B("yout")])
        out_tags.append(tg)

    P.sync_wait(out_tags)

    import contextlib
    with contextlib.ExitStack() as stack:
        for k in P.semkeys:
            P.sems[k] = stack.enter_context(nc.semaphore(k))
        with nc.Block() as block:
            @block.sync
            def _(eng):
                for f in P.ops["sync"]:
                    f(eng)

            @block.vector
            def _(eng):
                for f in P.ops["vector"]:
                    f(eng)

            @block.scalar
            def _(eng):
                for f in P.ops["scalar"]:
                    f(eng)

            @block.tensor
            def _(eng):
                for f in P.ops["tensor"]:
                    f(eng)

            @block.gpsimd
            def _(eng):
                for f in P.ops["gpsimd"]:
                    f(eng)
    return nc


_NC = None


def kernel(L0_attn_x, L0_attn_W_in, L0_attn_b_in, L0_attn_conv_w, L0_attn_conv_b,
           L0_attn_W_bc, L0_attn_b_bc, L0_attn_log_decay, L0_attn_W_out, L0_attn_b_out,
           L0_ffn_w_gate, L0_ffn_w_up, L0_ffn_w_down):
    global _NC
    if _NC is None:
        _NC = _build_nc()
    nc = _NC

    f32 = np.float32
    x = np.asarray(L0_attn_x, f32)
    com = {
        "wg": np.ascontiguousarray(np.asarray(L0_ffn_w_gate, f32)),
        "wu": np.ascontiguousarray(np.asarray(L0_ffn_w_up, f32)),
        "wd": np.ascontiguousarray(np.asarray(L0_ffn_w_down, f32)),
        "wi": np.ascontiguousarray(np.asarray(L0_attn_W_in, f32)),
        "wb": np.ascontiguousarray(np.asarray(L0_attn_W_bc, f32)),
        "wo": np.ascontiguousarray(np.asarray(L0_attn_W_out, f32)),
        "bin": np.ascontiguousarray(np.asarray(L0_attn_b_in, f32).reshape(KB, 128).T),
        "bbc": np.ascontiguousarray(np.asarray(L0_attn_b_bc, f32).reshape(KB, 128).T),
        "bout": np.ascontiguousarray(np.asarray(L0_attn_b_out, f32).reshape(KD, 128).T),
        "cwd": np.ascontiguousarray(np.asarray(L0_attn_conv_w, f32).reshape(KI, 128, 4).transpose(1, 2, 0)),
        "cbd": np.ascontiguousarray(np.asarray(L0_attn_conv_b, f32).reshape(KI, 128).T),
        "ldd": np.ascontiguousarray(np.asarray(L0_attn_log_decay, f32).reshape(KI, 128).T),
    }
    in_maps = []
    for i in range(8):
        xc = np.ascontiguousarray(x[2 * i:2 * i + 2].transpose(0, 2, 1))
        m = dict(com)
        m["x"] = xc
        in_maps.append(m)
    res = run_bass_kernel_spmd(nc, in_maps, core_ids=list(range(8)))
    outs = []
    for i in range(8):
        yc = np.asarray(res.results[i]["y"])
        outs.append(yc.transpose(0, 2, 1))
    return np.ascontiguousarray(np.concatenate(outs, axis=0))


# version 04
# speedup vs baseline: 1.1990x; 1.1990x over previous
import numpy as np
import concourse.bass as bass
import concourse.mybir as mybir
from concourse.bass_utils import run_bass_kernel_spmd

F32 = mybir.dt.float32
F32R = mybir.dt.float32r
BF16 = mybir.dt.bfloat16
ALU = mybir.AluOpType
ACT = mybir.ActivationFunctionType
AX = mybir.AxisListType.X

D, DI, DBC, DH = 1024, 2048, 4096, 2816
KD, KI, KB, KH = D // 128, DI // 128, DBC // 128, DH // 128
T = 512
NSEQ, SEQ = 2, 2048
NT_SEQ = SEQ // T
NT = NSEQ * NT_SEQ
SECT_H = 11
NSEC_H = KH // SECT_H
SECT_Y = 8
NSEC_Y = KI // SECT_Y

ENGS = ("sync", "vector", "scalar", "gpsimd", "tensor")
SELF = {"vector": "vS", "scalar": "sS", "gpsimd": "gS"}


class Buf:
    __slots__ = ("w", "rs")

    def __init__(self):
        self.w = {}
        self.rs = {}

    @staticmethod
    def _merge(d, tag):
        if tag is None:
            return
        k, v = tag
        if d.get(k, 0) < v:
            d[k] = v


class Plan:
    def __init__(self, nc):
        self.nc = nc
        self.ops = {e: [] for e in ENGS}
        self.cnt = {"vS": 0, "sS": 0, "gS": 0, "tS": 0, "qa": 0, "qw1": 0, "qw2": 0}
        self.seen = {e: {} for e in ENGS}
        self.semkeys = ["vS", "sS", "gS", "tS", "qa", "qw1", "qw2"]
        self.sems = {}

    def _waits(self, eng, bufs_r, bufs_w, extra=()):
        tags = {}
        for b in bufs_r:
            for k, v in b.w.items():
                if tags.get(k, 0) < v:
                    tags[k] = v
        for b in bufs_w:
            for d in (b.w, b.rs):
                for k, v in d.items():
                    if tags.get(k, 0) < v:
                        tags[k] = v
        for k, v in extra:
            if tags.get(k, 0) < v:
                tags[k] = v
        out = []
        seen = self.seen[eng]
        skip = SELF.get(eng)
        for k, v in tags.items():
            if k == skip:
                continue
            if seen.get(k, 0) >= v:
                continue
            seen[k] = v
            out.append((k, v))
        return out

    def _post(self, tag, bufs_r, bufs_w):
        for b in bufs_r:
            Buf._merge(b.rs, tag)
        for b in bufs_w:
            b.w = {}
            b.rs = {}
            Buf._merge(b.w, tag)

    def op(self, eng, fn, r=(), w=(), extra=()):
        sk = SELF[eng]
        waits = self._waits(eng, r, w, extra)
        n = self.cnt[sk]
        self.cnt[sk] = n + 1

        def emit(e, fn=fn, waits=waits, n=n, sk=sk):
            for k, v in waits:
                e.wait_ge(self.sems[k], v)
            ins = fn(e)
            if n > 0:
                ins._wait_ge(self.sems[sk], n)
            ins.then_inc(self.sems[sk], 1)

        self.ops[eng].append(emit)
        tag = (sk, n + 1)
        self._post(tag, r, w)
        return tag

    def tgroup(self, fns, r=(), w=()):
        waits = self._waits("tensor", r, w)
        n = self.cnt["tS"] + 1
        self.cnt["tS"] = n

        def emit(e, fns=fns, waits=waits):
            for k, v in waits:
                e.wait_ge(self.sems[k], v)
            ins = None
            for fn in fns:
                ins = fn(e)
            ins.then_inc(self.sems["tS"], 1)

        self.ops["tensor"].append(emit)
        tag = ("tS", n)
        self._post(tag, r, w)
        return tag

    def dma(self, q, fn, r=(), w=()):
        waits = self._waits("sync", r, w)
        self.cnt[q] += 16
        val = self.cnt[q]

        def emit(e, fn=fn, waits=waits, q=q):
            for k, v in waits:
                e.wait_ge(self.sems[k], v)
            fn(e).then_inc(self.sems[q], 16)

        self.ops["sync"].append(emit)
        tag = (q, val)
        self._post(tag, r, w)
        return tag

    def sync_wait(self, tags):
        waits = self._waits("sync", (), (), extra=tags)

        def emit(e, waits=waits):
            for k, v in waits:
                e.wait_ge(self.sems[k], v)

        self.ops["sync"].append(emit)


def _build_nc():
    nc = bass.Bass("TRN2", target_bir_lowering=False)
    P = Plan(nc)

    xin = nc.dram_tensor("x", [NSEQ, D, SEQ], F32, kind="ExternalInput").ap()
    yout = nc.dram_tensor("y", [NSEQ, D, SEQ], F32, kind="ExternalOutput").ap()
    wg = nc.dram_tensor("wg", [DH, D], F32, kind="ExternalInput").ap()
    wu = nc.dram_tensor("wu", [DH, D], F32, kind="ExternalInput").ap()
    wd = nc.dram_tensor("wd", [D, DH], F32, kind="ExternalInput").ap()
    wi = nc.dram_tensor("wi", [D, DBC], F32, kind="ExternalInput").ap()
    wb = nc.dram_tensor("wb", [DI, DBC], F32, kind="ExternalInput").ap()
    wo = nc.dram_tensor("wo", [DI, D], F32, kind="ExternalInput").ap()
    bin_ = nc.dram_tensor("bin", [128, KB], F32, kind="ExternalInput").ap()
    bbc = nc.dram_tensor("bbc", [128, KB], F32, kind="ExternalInput").ap()
    bout = nc.dram_tensor("bout", [128, KD], F32, kind="ExternalInput").ap()
    cwd = nc.dram_tensor("cwd", [128, 4, KI], F32, kind="ExternalInput").ap()
    cbd = nc.dram_tensor("cbd", [128, KI], F32, kind="ExternalInput").ap()
    ldd = nc.dram_tensor("ldd", [128, KI], F32, kind="ExternalInput").ap()
    ident_d = nc.inline_tensor(np.eye(128, dtype=np.float32), name="ident").ap()

    SG = [nc.dram_tensor(f"SG{i}", [KH, 128, KD * 128], F32R, kind="Internal").ap() for i in range(2)]
    SU = [nc.dram_tensor(f"SU{i}", [KH, 128, KD * 128], F32R, kind="Internal").ap() for i in range(2)]
    SD = [nc.dram_tensor(f"SD{i}", [KD, 128, KH * 128], F32R, kind="Internal").ap() for i in range(2)]
    SI = [nc.dram_tensor(f"SI{i}", [KB, 128, KD * 128], F32R, kind="Internal").ap() for i in range(2)]
    SB = [nc.dram_tensor(f"SB{i}", [KB, 128, KI * 128], F32R, kind="Internal").ap() for i in range(2)]
    SO = [nc.dram_tensor(f"SO{i}", [KD, 128, KI * 128], F32R, kind="Internal").ap() for i in range(2)]

    A = nc.alloc_sbuf_tensor
    X = A("X", [128, KD * T], F32)
    AQ1 = A("AQ1", [128, KD * T], F32R)
    AQ2 = A("AQ2", [128, KD * T], F32R)
    BIG1 = A("BIG1", [128, KI * T], F32R)
    BIG2 = A("BIG2", [128, KI * T], F32R)
    SIGG = A("SIGG", [128, KI * T], BF16)
    CH1 = A("CH1", [128, T], F32)
    CH2 = A("CH2", [128, T], F32)
    QT = A("QT", [128, T], F32)
    QAB = A("QAB", [128, T], F32)
    QSG = A("QSG", [128, T], F32)
    QC = A("QC", [128, T], F32)
    QD = A("QD", [128, T], F32)
    UW = A("UW", [128, T + 4], F32)
    QBM = A("QBM", [128, 176], F32)
    QSC = A("QSC", [128, 176], F32)
    QIV = A("QIV", [128, 176], F32)
    SMU = A("SMU", [1, T], F32)
    SVR = A("SVR", [1, T], F32)
    ST1 = A("ST1", [1, T], F32)
    SY0 = A("SY0", [1, T], F32)
    SIV = A("SIV", [1, T], F32)
    MUr = A("MUr", [1, T], F32R)
    IVr = A("IVr", [1, T], F32R)
    XRr = A("XRr", [128, T], F32R)
    XRr2 = A("XRr2", [128, T], F32R)
    SQF = A("SQF", [128, T], F32)
    SQR = A("SQR", [128, T], F32R)
    SQR2 = A("SQR2", [128, T], F32R)
    UH = A("UH", [128, KI * 3], F32)
    SCAR = A("SCAR", [128, KI], F32)
    ID = A("ID", [128, 128], F32)
    ONESC = A("ONESC", [128, 1], F32R)
    ONESCf = A("ONESCf", [128, 1], F32)
    ONESR = A("ONESR", [1, 128], F32R)
    ONESRf = A("ONESRf", [1, 128], F32)
    BIN = A("BIN", [128, KB], F32)
    BBC = A("BBC", [128, KB], F32)
    BOUT = A("BOUT", [128, KD], F32)
    CW = A("CW", [128, 4 * KI], F32)
    CB = A("CB", [128, KI], F32)
    LDT = A("LDT", [128, KI], F32)
    AD = A("AD", [128, KI], F32)
    T16A = A("T16A", [128, KI], F32)
    T16B = A("T16B", [128, KI], F32)
    POOL = [A(f"POOL{i}", [128, KI * 128], F32R) for i in range(4)]

    PB = [nc.alloc_psum_tensor(f"pb{i}", [128, T], F32) for i in range(8)]

    bufs = {}

    def B(name):
        if name not in bufs:
            bufs[name] = Buf()
        return bufs[name]

    bX, bAQ1, bAQ2, bBIG1, bBIG2, bSIGG = B("X"), B("AQ1"), B("AQ2"), B("BIG1"), B("BIG2"), B("SIGG")
    bCH1, bQT, bQAB, bQSG, bQC, bQD = B("CH1"), B("QT"), B("QAB"), B("QSG"), B("QC"), B("QD")
    bCH2 = B("CH2")
    bUW, bQBM, bQSC, bQIV, bSM = B("UW"), B("QBM"), B("QSC"), B("QIV"), B("SM")
    bMUr, bIVr, bXRr, bSQF, bSQR = B("MUr"), B("IVr"), B("XRr"), B("SQF"), B("SQR")
    bXRr2, bSQR2 = B("XRr2"), B("SQR2")
    bUH, bSCAR = B("UH"), B("SCAR")
    bPOOL = [B(f"POOL{i}") for i in range(4)]
    bPB = [B(f"pb{i}") for i in range(8)]
    bSG = [B("SG0"), B("SG1")]
    bSU = [B("SU0"), B("SU1")]
    bSD = [B("SD0"), B("SD1")]
    bSI = [B("SI0"), B("SI1")]
    bSBc = [B("SBc0"), B("SBc1")]
    bSO = [B("SO0"), B("SO1")]
    bconsts = B("consts")

    P.dma("qa", lambda e: e.dma_start(out=ID[:], in_=ident_d), w=[bconsts])
    for src, dst in ((bin_, BIN), (bbc, BBC), (bout, BOUT), (cbd, CB), (ldd, LDT)):
        P.dma("qa", lambda e, s=src, d=dst: e.dma_start(out=d[:], in_=s), w=[bconsts])
    P.dma("qa", lambda e: e.dma_start(
        out=CW[:].rearrange("p (a j) -> p a j", a=4), in_=cwd), w=[bconsts])
    P.op("vector", lambda e: e.memset(ONESCf[:], 1.0), w=[bconsts])
    P.op("vector", lambda e: e.memset(ONESRf[:], 1.0), w=[bconsts])
    P.op("vector", lambda e: e.tensor_copy(ONESC[:], ONESCf[:]), r=[bconsts], w=[bconsts])
    P.op("vector", lambda e: e.tensor_copy(ONESR[:], ONESRf[:]), r=[bconsts], w=[bconsts])
    P.op("scalar", lambda e: e.activation(T16A[:], LDT[:], ACT.Abs, scale=-1.0), r=[bconsts], w=[B("T16A")])
    P.op("scalar", lambda e: e.activation(T16A[:], T16A[:], ACT.Exp), r=[B("T16A")], w=[B("T16A")])
    P.op("vector", lambda e: e.tensor_single_scalar(T16A[:], T16A[:], 1.0, ALU.add), r=[B("T16A")], w=[B("T16A")])
    P.op("scalar", lambda e: e.activation(T16A[:], T16A[:], ACT.Ln), r=[B("T16A")], w=[B("T16A")])
    P.op("vector", lambda e: e.tensor_single_scalar(T16B[:], LDT[:], 0.0, ALU.max), r=[bconsts], w=[B("T16B")])
    P.op("vector", lambda e: e.tensor_add(T16A[:], T16A[:], T16B[:]), r=[B("T16A"), B("T16B")], w=[B("T16A")])
    P.op("scalar", lambda e: e.activation(AD[:], T16A[:], ACT.Exp, scale=-1.0), r=[B("T16A")], w=[bconsts])

    def staircase(src_buf, nblk):
        v4 = QT[:].rearrange("p (g b k) -> p g b k", g=4, k=16)
        bm = QBM[:, 0:4 * nblk].rearrange("p (g b) -> p g b", g=4)
        P.op("vector", lambda e: e.tensor_reduce(bm, v4, AX, ALU.max, apply_absolute_value=True),
             r=[bQT], w=[bQBM])
        P.op("vector", lambda e: e.tensor_scalar(QSC[:, 0:4 * nblk], QBM[:, 0:4 * nblk],
                                                 1.0 / 6.0, 1e-12, ALU.mult, ALU.max),
             r=[bQBM], w=[bQSC])
        P.op("vector", lambda e: e.reciprocal(QIV[:, 0:4 * nblk], QSC[:, 0:4 * nblk]),
             r=[bQSC], w=[bQIV])
        ivb = QIV[:, 0:4 * nblk].rearrange("p (g b) -> p g b", g=4).unsqueeze(3).broadcast_to([128, 4, nblk, 16])
        P.op("vector", lambda e: e.tensor_tensor(v4, v4, ivb, ALU.mult), r=[bQIV], w=[bQT])
        P.op("scalar", lambda e: e.activation(QAB[:], QT[:], ACT.Abs), r=[bQT], w=[bQAB])
        P.op("scalar", lambda e: e.activation(QSG[:], QT[:], ACT.Sign), r=[bQT], w=[bQSG])
        P.op("vector", lambda e: e.tensor_single_scalar(QC[:], QAB[:], 0.25, ALU.is_gt), r=[bQAB], w=[bQC])
        for th in (0.75, 1.25, 1.75):
            P.op("vector", lambda e, th=th: e.scalar_tensor_tensor(QC[:], QAB[:], th, QC[:], ALU.is_gt, ALU.add),
                 r=[bQAB], w=[bQC])
        P.op("vector", lambda e: e.tensor_single_scalar(QD[:], QAB[:], 2.5, ALU.is_gt), r=[bQAB], w=[bQD])
        P.op("vector", lambda e: e.scalar_tensor_tensor(QD[:], QAB[:], 3.5, QD[:], ALU.is_gt, ALU.add),
             r=[bQAB], w=[bQD])
        P.op("vector", lambda e: e.scalar_tensor_tensor(QC[:], QC[:], 0.5, QD[:], ALU.mult, ALU.add),
             r=[bQD], w=[bQC])
        P.op("vector", lambda e: e.tensor_scalar(QD[:], QAB[:], 5.0, 2.0, ALU.is_gt, ALU.mult),
             r=[bQAB], w=[bQD])
        P.op("vector", lambda e: e.tensor_add(QC[:], QC[:], QD[:]), r=[bQD], w=[bQC])
        P.op("vector", lambda e: e.tensor_mul(QC[:], QC[:], QSG[:]), r=[bQSG], w=[bQC])
        scb = QSC[:, 0:4 * nblk].rearrange("p (g b) -> p g b", g=4).unsqueeze(3).broadcast_to([128, 4, nblk, 16])
        qc4 = QC[:].rearrange("p (g b k) -> p g b k", g=4, k=16)
        P.op("vector", lambda e: e.tensor_tensor(v4, qc4, scb, ALU.mult), r=[bQC, bQSC], w=[bQT])

    def quant_front(src_sb, src_buf, pbA):
        bA = bPB[pbA]
        P.tgroup([(lambda e, g=g, pbA=pbA, src_sb=src_sb: e.transpose(
            PB[pbA][:, g * 128:(g + 1) * 128], src_sb[:, g * 128:(g + 1) * 128], ID[:]))
                  for g in range(4)], r=[src_buf, bconsts], w=[bA])
        P.op("scalar", lambda e, pbA=pbA: e.copy(QT[:], PB[pbA][:]), r=[bA], w=[bQT])

    def quant_back(q1_ap, q2_ap, dst1_buf, dst2_buf, nblk, pbB):
        bB = bPB[pbB]
        staircase(None, nblk)
        P.tgroup([(lambda e, g=g, pbB=pbB: e.transpose(
            PB[pbB][:, g * 128:(g + 1) * 128], QT[:, g * 128:(g + 1) * 128], ID[:]))
                  for g in range(4)], r=[bQT, bconsts], w=[bB])
        P.op("vector", lambda e, pbB=pbB: e.tensor_copy(q1_ap, PB[pbB][:]), r=[bB], w=[dst1_buf])
        P.op("vector", lambda e, pbB=pbB: e.tensor_tensor(q2_ap, PB[pbB][:], q1_ap.bitcast(F32),
                                                          ALU.subtract),
             r=[bB, dst1_buf], w=[dst2_buf])

    def quant_tile(src_sb, src_buf, q1_ap, q2_ap, dst1_buf, dst2_buf, nblk, pbA, pbB):
        quant_front(src_sb, src_buf, pbA)
        quant_back(q1_ap, q2_ap, dst1_buf, dst2_buf, nblk, pbB)

    def prep_ffn_weight(src, s1, s2, bs1, bs2, nrows_chunks, kfree):
        pt1 = AQ2[:, 0:512]
        pt2 = AQ2[:, 512:1024]
        for j in range(nrows_chunks):
            for c0 in range(0, kfree, 512):
                cw_ = min(512, kfree - c0)
                nb = cw_ // 16
                nk = cw_ // 128
                P.dma("qa", lambda e, j=j, c0=c0, cw_=cw_: e.dma_start(
                    out=QT[:, 0:cw_], in_=src[j * 128:(j + 1) * 128, c0:c0 + cw_]),
                    r=[bQT], w=[bQT])
                v16 = QT[:, 0:cw_].rearrange("p (b k) -> p b k", k=16)
                bm = QBM[:, 0:nb]
                P.op("vector", lambda e, v16=v16, bm=bm: e.tensor_reduce(
                    bm, v16, AX, ALU.max, apply_absolute_value=True), r=[bQT], w=[bQBM])
                P.op("vector", lambda e, bm=bm, nb=nb: e.tensor_scalar(
                    QSC[:, 0:nb], bm, 1.0 / 6.0, 1e-12, ALU.mult, ALU.max), r=[bQBM], w=[bQSC])
                P.op("vector", lambda e, nb=nb: e.reciprocal(QIV[:, 0:nb], QSC[:, 0:nb]),
                     r=[bQSC], w=[bQIV])
                ivb = QIV[:, 0:nb].unsqueeze(2).broadcast_to([128, nb, 16])
                scb = QSC[:, 0:nb].unsqueeze(2).broadcast_to([128, nb, 16])
                P.op("vector", lambda e, v16=v16, ivb=ivb: e.tensor_tensor(
                    v16, v16, ivb, ALU.mult), r=[bQIV], w=[bQT])
                P.op("scalar", lambda e, cw_=cw_: e.activation(QSG[:, 0:cw_], QT[:, 0:cw_], ACT.Sign),
                     r=[bQT], w=[bQSG])
                P.op("scalar", lambda e, cw_=cw_: e.activation(QAB[:, 0:cw_], QT[:, 0:cw_], ACT.Abs),
                     r=[bQT], w=[bQAB])
                P.op("vector", lambda e, cw_=cw_: e.tensor_single_scalar(
                    QC[:, 0:cw_], QAB[:, 0:cw_], 0.25, ALU.is_gt), r=[bQAB], w=[bQC])
                for th in (0.75, 1.25, 1.75):
                    P.op("vector", lambda e, th=th, cw_=cw_: e.scalar_tensor_tensor(
                        QC[:, 0:cw_], QAB[:, 0:cw_], th, QC[:, 0:cw_], ALU.is_gt, ALU.add),
                        r=[bQAB], w=[bQC])
                P.op("vector", lambda e, cw_=cw_: e.tensor_single_scalar(
                    QD[:, 0:cw_], QAB[:, 0:cw_], 2.5, ALU.is_gt), r=[bQAB], w=[bQD])
                P.op("vector", lambda e, cw_=cw_: e.scalar_tensor_tensor(
                    QD[:, 0:cw_], QAB[:, 0:cw_], 3.5, QD[:, 0:cw_], ALU.is_gt, ALU.add),
                    r=[bQAB], w=[bQD])
                P.op("vector", lambda e, cw_=cw_: e.scalar_tensor_tensor(
                    QC[:, 0:cw_], QC[:, 0:cw_], 0.5, QD[:, 0:cw_], ALU.mult, ALU.add),
                    r=[bQD], w=[bQC])
                P.op("vector", lambda e, cw_=cw_: e.tensor_scalar(
                    QD[:, 0:cw_], QAB[:, 0:cw_], 5.0, 2.0, ALU.is_gt, ALU.mult),
                    r=[bQAB], w=[bQD])
                P.op("vector", lambda e, cw_=cw_: e.tensor_add(
                    QC[:, 0:cw_], QC[:, 0:cw_], QD[:, 0:cw_]), r=[bQD], w=[bQC])
                P.op("vector", lambda e, cw_=cw_: e.tensor_mul(
                    QC[:, 0:cw_], QC[:, 0:cw_], QSG[:, 0:cw_]), r=[bQSG], w=[bQC])
                qc16 = QC[:, 0:cw_].rearrange("p (b k) -> p b k", k=16)
                P.op("vector", lambda e, qc16=qc16, scb=scb: e.tensor_tensor(
                    qc16, qc16, scb, ALU.mult), r=[bQSC], w=[bQC])
                P.tgroup([(lambda e, kk=kk: e.transpose(PB[0][:, kk * 128:(kk + 1) * 128],
                                                        QC[:, kk * 128:(kk + 1) * 128], ID[:]))
                          for kk in range(nk)], r=[bQC, bconsts], w=[bPB[0]])
                wdt = nk * 128
                P.op("vector", lambda e, wdt=wdt: e.tensor_copy(pt1[:, 0:wdt], PB[0][:, 0:wdt]),
                     r=[bPB[0]], w=[bAQ2])
                P.op("vector", lambda e, wdt=wdt: e.tensor_tensor(
                    pt2[:, 0:wdt], PB[0][:, 0:wdt], pt1[:, 0:wdt].bitcast(F32), ALU.subtract),
                    r=[bPB[0], bAQ2], w=[bAQ2])
                P.dma("qa", lambda e, j=j, c0=c0, wdt=wdt: e.dma_start(
                    out=s1[j, :, c0:c0 + wdt], in_=pt1[:, 0:wdt]), r=[bAQ2], w=[bs1])
                P.dma("qa", lambda e, j=j, c0=c0, wdt=wdt: e.dma_start(
                    out=s2[j, :, c0:c0 + wdt], in_=pt2[:, 0:wdt]), r=[bAQ2], w=[bs2])

    prep_ffn_weight(wg, SG[0], SG[1], bSG[0], bSG[1], KH, D)
    prep_ffn_weight(wu, SU[0], SU[1], bSU[0], bSU[1], KH, D)
    prep_ffn_weight(wd, SD[0], SD[1], bSD[0], bSD[1], KD, DH)

    def prep_mm_weight(src, s1, s2, bs1, bs2, kchunks, m):
        mb = m // 128
        w1 = AQ1[:, 0:512]
        w2 = AQ2[:, 0:512]
        for k in range(kchunks):
            for b0 in range(0, mb, 4):
                bw = min(4, mb - b0)
                cw_ = bw * 128
                P.dma("qa", lambda e, k=k, b0=b0, cw_=cw_: e.dma_start(
                    out=QT[:, 0:cw_], in_=src[k * 128:(k + 1) * 128, b0 * 128:b0 * 128 + cw_]),
                    r=[bQT], w=[bQT])
                P.op("vector", lambda e, cw_=cw_: e.tensor_copy(w1[:, 0:cw_], QT[:, 0:cw_]),
                     r=[bQT], w=[bAQ1])
                P.op("vector", lambda e, cw_=cw_: e.tensor_tensor(
                    w2[:, 0:cw_], QT[:, 0:cw_], w1[:, 0:cw_].bitcast(F32), ALU.subtract),
                    r=[bQT, bAQ1], w=[bAQ2])
                for ws, bws, sd, bd in ((w1, bAQ1, s1, bs1), (w2, bAQ2, s2, bs2)):
                    P.dma("qa", lambda e, ws=ws, sd=sd, k=k, b0=b0, bw=bw, cw_=cw_: e.dma_start(
                        out=sd[b0:b0 + bw, :, k * 128:(k + 1) * 128].transpose([1, 0, 2]),
                        in_=ws[:, 0:cw_].rearrange("p (b q) -> p b q", b=bw)),
                        r=[bws], w=[bd])

    prep_mm_weight(wi, SI[0], SI[1], bSI[0], bSI[1], KD, DBC)
    prep_mm_weight(wb, SB[0], SB[1], bSBc[0], bSBc[1], KI, DBC)
    prep_mm_weight(wo, SO[0], SO[1], bSO[0], bSO[1], KI, D)

    def layernorm_prep():
        for ch in range(KD):
            xc = X[:, ch * T:(ch + 1) * T]
            xr, bxr = (XRr, bXRr) if ch % 2 == 0 else (XRr2, bXRr2)
            sq, bsq = (SQR, bSQR) if ch % 2 == 0 else (SQR2, bSQR2)
            P.op("vector", lambda e, xc=xc, xr=xr: e.tensor_copy(xr[:], xc), r=[bX], w=[bxr])
            P.tgroup([(lambda e, ch=ch, xr=xr: e.matmul(PB[6][0:1, :], ONESC[:], xr[:],
                                                        start=(ch == 0), stop=(ch == KD - 1),
                                                        skip_group_check=True))],
                     r=[bxr, bconsts], w=[bPB[6]])
            P.op("vector", lambda e, xc=xc: e.tensor_tensor(SQF[:], xc, xc, ALU.mult), r=[bX], w=[bSQF])
            P.op("vector", lambda e, sq=sq: e.tensor_copy(sq[:], SQF[:]), r=[bSQF], w=[bsq])
            P.tgroup([(lambda e, ch=ch, sq=sq: e.matmul(PB[7][0:1, :], ONESC[:], sq[:],
                                                        start=(ch == 0), stop=(ch == KD - 1),
                                                        skip_group_check=True))],
                     r=[bsq, bconsts], w=[bPB[7]])
        mu, var, t1, y0, ivs = SMU[:], SVR[:], ST1[:], SY0[:], SIV[:]
        P.op("vector", lambda e: e.tensor_single_scalar(mu, PB[6][0:1, :], 1.0 / D, ALU.mult),
             r=[bPB[6]], w=[bSM])
        P.op("vector", lambda e: e.tensor_single_scalar(var, PB[7][0:1, :], 1.0 / D, ALU.mult),
             r=[bPB[7]], w=[bSM])
        P.op("vector", lambda e: e.tensor_mul(t1, mu, mu), r=[bSM], w=[bSM])
        P.op("vector", lambda e: e.tensor_sub(var, var, t1), r=[bSM], w=[bSM])
        P.op("vector", lambda e: e.tensor_single_scalar(var, var, 1e-5, ALU.add), r=[bSM], w=[bSM])
        P.op("vector", lambda e: e.reciprocal(t1, var), r=[bSM], w=[bSM])
        P.op("scalar", lambda e: e.activation(y0, t1, ACT.Sqrt), r=[bSM], w=[bSM])
        P.op("vector", lambda e: e.tensor_mul(ivs, y0, y0), r=[bSM], w=[bSM])
        P.op("vector", lambda e: e.tensor_mul(ivs, ivs, var), r=[bSM], w=[bSM])
        P.op("vector", lambda e: e.tensor_scalar(ivs, ivs, -0.5, 1.5, ALU.mult, ALU.add), r=[bSM], w=[bSM])
        P.op("vector", lambda e: e.tensor_mul(ivs, ivs, y0), r=[bSM], w=[bSM])
        P.op("vector", lambda e: e.tensor_copy(MUr[:], mu), r=[bSM], w=[bMUr])
        P.op("vector", lambda e: e.tensor_copy(IVr[:], ivs), r=[bSM], w=[bIVr])
        P.tgroup([(lambda e: e.matmul(PB[4][:], ONESR[:], MUr[:], start=True, stop=True))],
                 r=[bMUr, bconsts], w=[bPB[4]])
        P.tgroup([(lambda e: e.matmul(PB[5][:], ONESR[:], IVr[:], start=True, stop=True))],
                 r=[bIVr, bconsts], w=[bPB[5]])

    def ln_chunk(ch):
        xc = X[:, ch * T:(ch + 1) * T]
        P.op("vector", lambda e, xc=xc: e.scalar_tensor_tensor(CH1[:], xc, 1.0, PB[4][:],
                                                               ALU.bypass, ALU.subtract),
             r=[bX, bPB[4]], w=[bCH1])
        P.op("vector", lambda e: e.tensor_tensor(CH1[:], CH1[:], PB[5][:], ALU.mult),
             r=[bPB[5]], w=[bCH1])

    qwkey_f = "qw1"

    pool_i = [0]

    def next_slot():
        i = pool_i[0] % 4
        pool_i[0] += 1
        return i

    def ffn_sublayer():
        layernorm_prep()
        for ch in range(KD):
            ln_chunk(ch)
            quant_tile(CH1, bCH1, AQ1[:, ch * T:(ch + 1) * T], AQ2[:, ch * T:(ch + 1) * T],
                       bAQ1, bAQ2, KD, 0, 1)
        for s in range(NSEC_H):
            ms = list(range(s * SECT_H, (s + 1) * SECT_H))
            pend = None
            def flush_front(pend):
                quant_front(pend[1], pend[2], 0)
            def flush_back(pend):
                ci = pend[0]
                quant_back(BIG1[:, ci * T:(ci + 1) * T], BIG2[:, ci * T:(ci + 1) * T],
                           bBIG1, bBIG2, KD, 1)
            for ii, mb in enumerate(ms):
                if pend is not None:
                    flush_front(pend)
                sl_a, sl_b = next_slot(), next_slot()
                pg1, pu1 = POOL[sl_a][:, 0:KD * 128], POOL[sl_a][:, KD * 128:2 * KD * 128]
                pg2, pu2 = POOL[sl_b][:, 0:KD * 128], POOL[sl_b][:, KD * 128:2 * KD * 128]
                P.dma(qwkey_f, lambda e, mb=mb, d=pg1: e.dma_start(out=d, in_=SG[0][mb]), r=[bSG[0]], w=[bPOOL[sl_a]])
                P.dma(qwkey_f, lambda e, mb=mb, d=pu1: e.dma_start(out=d, in_=SU[0][mb]), r=[bSU[0]], w=[bPOOL[sl_a]])
                P.dma(qwkey_f, lambda e, mb=mb, d=pg2: e.dma_start(out=d, in_=SG[1][mb]), r=[bSG[1]], w=[bPOOL[sl_b]])
                P.dma(qwkey_f, lambda e, mb=mb, d=pu2: e.dma_start(out=d, in_=SU[1][mb]), r=[bSU[1]], w=[bPOOL[sl_b]])
                gb, ub = 2 + 2 * (ii % 2), 3 + 2 * (ii % 2)
                for (p1, p2, pbi) in ((pg1, pg2, gb), (pu1, pu2, ub)):
                    fns = []
                    for k in range(KD):
                        st1 = p1[:, k * 128:(k + 1) * 128]
                        st2 = p2[:, k * 128:(k + 1) * 128]
                        m1 = AQ1[:, k * T:(k + 1) * T]
                        m2 = AQ2[:, k * T:(k + 1) * T]
                        fns.append(lambda e, st1=st1, m1=m1, k=k, pbi=pbi: e.matmul(
                            PB[pbi][:], st1, m1, start=(k == 0), stop=False))
                        fns.append(lambda e, st1=st1, m2=m2, pbi=pbi: e.matmul(
                            PB[pbi][:], st1, m2, start=False, stop=False))
                        fns.append(lambda e, st2=st2, m1=m1, k=k, pbi=pbi: e.matmul(
                            PB[pbi][:], st2, m1, start=False, stop=(k == KD - 1)))
                    P.tgroup(fns, r=[bAQ1, bAQ2, bPOOL[sl_a], bPOOL[sl_b]], w=[bPB[pbi]])
                if pend is not None:
                    flush_back(pend)
                chs, chb = (CH1, bCH1) if ii % 2 == 0 else (CH2, bCH2)
                P.op("scalar", lambda e, chs=chs, gb=gb: e.activation(chs[:], PB[gb][:], ACT.Silu),
                     r=[bPB[gb]], w=[chb])
                P.op("vector", lambda e, chs=chs, ub=ub: e.tensor_tensor(chs[:], chs[:], PB[ub][:],
                                                                         ALU.mult),
                     r=[bPB[ub]], w=[chb])
                pend = (ii, chs, chb)
            flush_front(pend)
            flush_back(pend)
            for mo in range(KD):
                sl = next_slot()
                sl2 = next_slot()
                pd1 = POOL[sl][:, 0:SECT_H * 128]
                pd2 = POOL[sl2][:, 0:SECT_H * 128]
                c0 = s * SECT_H * 128
                P.dma(qwkey_f, lambda e, mo=mo, c0=c0, d=pd1: e.dma_start(
                    out=d, in_=SD[0][mo, :, c0:c0 + SECT_H * 128]), r=[bSD[0]], w=[bPOOL[sl]])
                P.dma(qwkey_f, lambda e, mo=mo, c0=c0, d=pd2: e.dma_start(
                    out=d, in_=SD[1][mo, :, c0:c0 + SECT_H * 128]), r=[bSD[1]], w=[bPOOL[sl2]])
                pbi = 2 + (mo % 4)
                fns = []
                for ci in range(SECT_H):
                    st1 = pd1[:, ci * 128:(ci + 1) * 128]
                    st2 = pd2[:, ci * 128:(ci + 1) * 128]
                    m1 = BIG1[:, ci * T:(ci + 1) * T]
                    m2 = BIG2[:, ci * T:(ci + 1) * T]
                    fns.append(lambda e, st1=st1, m1=m1, ci=ci, pbi=pbi: e.matmul(
                        PB[pbi][:], st1, m1, start=(ci == 0), stop=False))
                    fns.append(lambda e, st1=st1, m2=m2, pbi=pbi: e.matmul(
                        PB[pbi][:], st1, m2, start=False, stop=False))
                    fns.append(lambda e, st2=st2, m1=m1, ci=ci, pbi=pbi: e.matmul(
                        PB[pbi][:], st2, m1, start=False, stop=(ci == SECT_H - 1)))
                P.tgroup(fns, r=[bBIG1, bBIG2, bPOOL[sl], bPOOL[sl2]], w=[bPB[pbi]])
                xc = X[:, mo * T:(mo + 1) * T]
                P.op("vector", lambda e, xc=xc, pbi=pbi: e.scalar_tensor_tensor(
                    xc, PB[pbi][:], 0.5, xc, ALU.mult, ALU.add), r=[bPB[pbi]], w=[bX])

    def mamba_sublayer(c_pos):
        layernorm_prep()
        for ch in range(KD):
            ln_chunk(ch)
            P.op("vector", lambda e, ch=ch: e.tensor_copy(AQ1[:, ch * T:(ch + 1) * T], CH1[:]),
                 r=[bCH1], w=[bAQ1])
            P.op("vector", lambda e, ch=ch: e.tensor_tensor(AQ2[:, ch * T:(ch + 1) * T], CH1[:],
                                                            AQ1[:, ch * T:(ch + 1) * T].bitcast(F32),
                                                            ALU.subtract),
                 r=[bCH1, bAQ1], w=[bAQ2])
        for j in range(KI):
            sl_a, sl_b = next_slot(), next_slot()
            piu1, piu2 = POOL[sl_a][:, 0:KD * 128], POOL[sl_a][:, KD * 128:2 * KD * 128]
            pig1, pig2 = POOL[sl_b][:, 0:KD * 128], POOL[sl_b][:, KD * 128:2 * KD * 128]
            P.dma("qw2", lambda e, j=j, d=piu1: e.dma_start(out=d, in_=SI[0][j]), r=[bSI[0]], w=[bPOOL[sl_a]])
            P.dma("qw2", lambda e, j=j, d=piu2: e.dma_start(out=d, in_=SI[1][j]), r=[bSI[1]], w=[bPOOL[sl_a]])
            P.dma("qw2", lambda e, j=j, d=pig1: e.dma_start(out=d, in_=SI[0][KI + j]), r=[bSI[0]], w=[bPOOL[sl_b]])
            P.dma("qw2", lambda e, j=j, d=pig2: e.dma_start(out=d, in_=SI[1][KI + j]), r=[bSI[1]], w=[bPOOL[sl_b]])
            ub_, gb_ = 2 + 2 * (j % 2), 3 + 2 * (j % 2)
            for (p1, p2, pbi) in ((piu1, piu2, ub_), (pig1, pig2, gb_)):
                fns = []
                for k in range(KD):
                    st1 = p1[:, k * 128:(k + 1) * 128]
                    st2 = p2[:, k * 128:(k + 1) * 128]
                    m1 = AQ1[:, k * T:(k + 1) * T]
                    m2 = AQ2[:, k * T:(k + 1) * T]
                    fns.append(lambda e, st1=st1, m1=m1, k=k, pbi=pbi: e.matmul(
                        PB[pbi][:], st1, m1, start=(k == 0), stop=False))
                    fns.append(lambda e, st1=st1, m2=m2, pbi=pbi: e.matmul(
                        PB[pbi][:], st1, m2, start=False, stop=False))
                    fns.append(lambda e, st2=st2, m1=m1, k=k, pbi=pbi: e.matmul(
                        PB[pbi][:], st2, m1, start=False, stop=(k == KD - 1)))
                P.tgroup(fns, r=[bAQ1, bAQ2, bPOOL[sl_a], bPOOL[sl_b]], w=[bPB[pbi]])
            if c_pos == 0:
                P.op("vector", lambda e: e.memset(UW[:, 0:3], 0.0), w=[bUW])
            else:
                P.op("vector", lambda e, j=j: e.tensor_copy(UW[:, 0:3], UH[:, j * 3:j * 3 + 3]),
                     r=[bUH], w=[bUW])
            P.op("vector", lambda e, j=j, ub_=ub_: e.tensor_scalar(UW[:, 3:3 + T], PB[ub_][:],
                                                          BIN[:, j:j + 1], None, ALU.add),
                 r=[bPB[ub_], bconsts], w=[bUW])
            P.op("vector", lambda e, j=j: e.tensor_copy(UH[:, j * 3:j * 3 + 3], UW[:, T:T + 3]),
                 r=[bUW], w=[bUH])
            P.op("vector", lambda e, j=j: e.tensor_scalar(QC[:], UW[:, 0:T], CW[:, 0 * KI + j:0 * KI + j + 1],
                                                          None, ALU.mult),
                 r=[bUW, bconsts], w=[bQC])
            P.op("vector", lambda e, j=j: e.scalar_tensor_tensor(QD[:], UW[:, 1:1 + T],
                                                                 CW[:, 1 * KI + j:1 * KI + j + 1],
                                                                 QC[:], ALU.mult, ALU.add),
                 r=[bUW, bQC, bconsts], w=[bQD])
            P.op("vector", lambda e, j=j: e.scalar_tensor_tensor(QC[:], UW[:, 2:2 + T],
                                                                 CW[:, 2 * KI + j:2 * KI + j + 1],
                                                                 QD[:], ALU.mult, ALU.add),
                 r=[bUW, bQD, bconsts], w=[bQC])
            P.op("vector", lambda e, j=j: e.scalar_tensor_tensor(QD[:], UW[:, 3:3 + T],
                                                                 CW[:, 3 * KI + j:3 * KI + j + 1],
                                                                 QC[:], ALU.mult, ALU.add),
                 r=[bUW, bQC, bconsts], w=[bQD])
            P.op("scalar", lambda e, j=j: e.activation(QT[:], QD[:], ACT.Silu, bias=CB[:, j:j + 1]),
                 r=[bQD, bconsts], w=[bQT])
            P.op("vector", lambda e, j=j: e.tensor_copy(BIG1[:, j * T:(j + 1) * T], QT[:]),
                 r=[bQT], w=[bBIG1])
            P.op("vector", lambda e, j=j: e.tensor_tensor(BIG2[:, j * T:(j + 1) * T], QT[:],
                                                          BIG1[:, j * T:(j + 1) * T].bitcast(F32),
                                                          ALU.subtract),
                 r=[bQT, bBIG1], w=[bBIG2])
            P.op("scalar", lambda e, j=j, gb_=gb_: e.activation(SIGG[:, j * T:(j + 1) * T], PB[gb_][:],
                                                       ACT.Sigmoid, bias=BIN[:, KI + j:KI + j + 1]),
                 r=[bPB[gb_], bconsts], w=[bSIGG])

        for j in range(KI):
            bb_, cb_ = 2 + 2 * (j % 2), 3 + 2 * (j % 2)
            sl_a, sl_b = next_slot(), next_slot()
            pb1s, pb2s = POOL[sl_a][:], POOL[sl_b][:]
            P.dma("qw2", lambda e, j=j, d=pb1s: e.dma_start(out=d, in_=SB[0][j]), r=[bSBc[0]], w=[bPOOL[sl_a]])
            P.dma("qw2", lambda e, j=j, d=pb2s: e.dma_start(out=d, in_=SB[1][j]), r=[bSBc[1]], w=[bPOOL[sl_b]])
            fns = []
            for k in range(KI):
                st1 = pb1s[:, k * 128:(k + 1) * 128]
                st2 = pb2s[:, k * 128:(k + 1) * 128]
                m1 = BIG1[:, k * T:(k + 1) * T]
                m2 = BIG2[:, k * T:(k + 1) * T]
                fns.append(lambda e, st1=st1, m1=m1, k=k, bb_=bb_: e.matmul(
                    PB[bb_][:], st1, m1, start=(k == 0), stop=False))
                fns.append(lambda e, st1=st1, m2=m2, bb_=bb_: e.matmul(
                    PB[bb_][:], st1, m2, start=False, stop=False))
                fns.append(lambda e, st2=st2, m1=m1, k=k, bb_=bb_: e.matmul(
                    PB[bb_][:], st2, m1, start=False, stop=(k == KI - 1)))
            P.tgroup(fns, r=[bBIG1, bBIG2, bPOOL[sl_a], bPOOL[sl_b]], w=[bPB[bb_]])
            sl_c, sl_d = next_slot(), next_slot()
            pc1s, pc2s = POOL[sl_c][:], POOL[sl_d][:]
            P.dma("qw2", lambda e, j=j, d=pc1s: e.dma_start(out=d, in_=SB[0][KI + j]), r=[bSBc[0]], w=[bPOOL[sl_c]])
            P.dma("qw2", lambda e, j=j, d=pc2s: e.dma_start(out=d, in_=SB[1][KI + j]), r=[bSBc[1]], w=[bPOOL[sl_d]])
            fns = []
            for k in range(KI):
                st1 = pc1s[:, k * 128:(k + 1) * 128]
                st2 = pc2s[:, k * 128:(k + 1) * 128]
                m1 = BIG1[:, k * T:(k + 1) * T]
                m2 = BIG2[:, k * T:(k + 1) * T]
                fns.append(lambda e, st1=st1, m1=m1, k=k, cb_=cb_: e.matmul(
                    PB[cb_][:], st1, m1, start=(k == 0), stop=False))
                fns.append(lambda e, st1=st1, m2=m2, cb_=cb_: e.matmul(
                    PB[cb_][:], st1, m2, start=False, stop=False))
                fns.append(lambda e, st2=st2, m1=m1, k=k, cb_=cb_: e.matmul(
                    PB[cb_][:], st2, m1, start=False, stop=(k == KI - 1)))
            P.tgroup(fns, r=[bBIG1, bBIG2, bPOOL[sl_c], bPOOL[sl_d]], w=[bPB[cb_]])
            P.op("vector", lambda e, j=j, bb_=bb_: e.tensor_scalar(QT[:], PB[bb_][:], BBC[:, j:j + 1],
                                                          None, ALU.add),
                 r=[bPB[bb_], bconsts], w=[bQT])
            P.op("vector", lambda e, j=j, cb_=cb_: e.tensor_scalar(QAB[:], PB[cb_][:],
                                                          BBC[:, KI + j:KI + j + 1], None, ALU.add),
                 r=[bPB[cb_], bconsts], w=[bQAB])
            a_b = AD[:, j:j + 1].broadcast_to([128, T])
            init = 0.0 if (c_pos == 0) else SCAR[:, j:j + 1]
            extra_r = [bconsts] if c_pos == 0 else [bconsts, bSCAR]
            P.op("vector", lambda e, a_b=a_b, init=init: e.tensor_tensor_scan(
                QSG[:], a_b, QT[:], init, ALU.mult, ALU.add),
                 r=[bQT] + extra_r, w=[bQSG])
            P.op("vector", lambda e, j=j: e.tensor_copy(SCAR[:, j:j + 1], QSG[:, T - 1:T]),
                 r=[bQSG], w=[bSCAR])
            P.op("vector", lambda e: e.tensor_mul(QT[:], QSG[:], QAB[:]), r=[bQSG, bQAB], w=[bQT])
            P.op("vector", lambda e, j=j: e.tensor_tensor(QT[:], QT[:], SIGG[:, j * T:(j + 1) * T],
                                                          ALU.mult),
                 r=[bSIGG], w=[bQT])
            jq = j % SECT_Y
            P.op("vector", lambda e, jq=jq: e.tensor_copy(AQ1[:, jq * T:(jq + 1) * T], QT[:]),
                 r=[bQT], w=[bAQ1])
            P.op("vector", lambda e, jq=jq: e.tensor_tensor(AQ2[:, jq * T:(jq + 1) * T], QT[:],
                                                            AQ1[:, jq * T:(jq + 1) * T].bitcast(F32),
                                                            ALU.subtract),
                 r=[bQT, bAQ1], w=[bAQ2])
            if jq == SECT_Y - 1:
                sec = j // SECT_Y
                for mo in range(KD):
                    sl = next_slot()
                    po1 = POOL[sl][:, 0:SECT_Y * 128]
                    po2 = POOL[sl][:, SECT_Y * 128:2 * SECT_Y * 128]
                    c0 = sec * SECT_Y * 128
                    P.dma("qw2", lambda e, mo=mo, c0=c0, d=po1: e.dma_start(
                        out=d, in_=SO[0][mo, :, c0:c0 + SECT_Y * 128]), r=[bSO[0]], w=[bPOOL[sl]])
                    P.dma("qw2", lambda e, mo=mo, c0=c0, d=po2: e.dma_start(
                        out=d, in_=SO[1][mo, :, c0:c0 + SECT_Y * 128]), r=[bSO[1]], w=[bPOOL[sl]])
                    pbi = 0 + (mo % 2)
                    fns = []
                    for jj in range(SECT_Y):
                        st1 = po1[:, jj * 128:(jj + 1) * 128]
                        st2 = po2[:, jj * 128:(jj + 1) * 128]
                        m1 = AQ1[:, jj * T:(jj + 1) * T]
                        m2 = AQ2[:, jj * T:(jj + 1) * T]
                        fns.append(lambda e, st1=st1, m1=m1, jj=jj, pbi=pbi: e.matmul(
                            PB[pbi][:], st1, m1, start=(jj == 0), stop=False))
                        fns.append(lambda e, st1=st1, m2=m2, pbi=pbi: e.matmul(
                            PB[pbi][:], st1, m2, start=False, stop=False))
                        fns.append(lambda e, st2=st2, m1=m1, jj=jj, pbi=pbi: e.matmul(
                            PB[pbi][:], st2, m1, start=False, stop=(jj == SECT_Y - 1)))
                    P.tgroup(fns, r=[bAQ1, bAQ2, bPOOL[sl]], w=[bPB[pbi]])
                    xc = X[:, mo * T:(mo + 1) * T]
                    if sec == NSEC_Y - 1:
                        P.op("vector", lambda e, mo=mo, pbi=pbi: e.tensor_scalar(
                            QC[:], PB[pbi][:], BOUT[:, mo:mo + 1], None, ALU.add),
                             r=[bPB[pbi], bconsts], w=[bQC])
                        P.op("vector", lambda e, xc=xc: e.tensor_add(xc, xc, QC[:]), r=[bQC], w=[bX])
                    else:
                        P.op("vector", lambda e, xc=xc, pbi=pbi: e.scalar_tensor_tensor(
                            xc, PB[pbi][:], 1.0, xc, ALU.bypass, ALU.add), r=[bPB[pbi]], w=[bX])

    out_tags = []
    for t in range(NT):
        s, c_pos = t // NT_SEQ, t % NT_SEQ
        P.dma("qa", lambda e, s=s, c_pos=c_pos: e.dma_start(
            out=X[:].rearrange("p (ch q) -> p ch q", ch=KD),
            in_=xin[s, :, c_pos * T:(c_pos + 1) * T].rearrange("(ch p) q -> p ch q", p=128)),
            r=[bX], w=[bX])
        ffn_sublayer()
        mamba_sublayer(c_pos)
        ffn_sublayer()
        tg = P.dma("qa", lambda e, s=s, c_pos=c_pos: e.dma_start(
            out=yout[s, :, c_pos * T:(c_pos + 1) * T].rearrange("(ch p) q -> p ch q", p=128),
            in_=X[:].rearrange("p (ch q) -> p ch q", ch=KD)),
            r=[bX], w=[B("yout")])
        out_tags.append(tg)

    P.sync_wait(out_tags)

    import contextlib
    with contextlib.ExitStack() as stack:
        for k in P.semkeys:
            P.sems[k] = stack.enter_context(nc.semaphore(k))
        with nc.Block() as block:
            @block.sync
            def _(eng):
                for f in P.ops["sync"]:
                    f(eng)

            @block.vector
            def _(eng):
                for f in P.ops["vector"]:
                    f(eng)

            @block.scalar
            def _(eng):
                for f in P.ops["scalar"]:
                    f(eng)

            @block.tensor
            def _(eng):
                for f in P.ops["tensor"]:
                    f(eng)

            @block.gpsimd
            def _(eng):
                for f in P.ops["gpsimd"]:
                    f(eng)
    return nc


_NC = None


def kernel(L0_attn_x, L0_attn_W_in, L0_attn_b_in, L0_attn_conv_w, L0_attn_conv_b,
           L0_attn_W_bc, L0_attn_b_bc, L0_attn_log_decay, L0_attn_W_out, L0_attn_b_out,
           L0_ffn_w_gate, L0_ffn_w_up, L0_ffn_w_down):
    global _NC
    if _NC is None:
        _NC = _build_nc()
    nc = _NC

    f32 = np.float32
    x = np.asarray(L0_attn_x, f32)
    com = {
        "wg": np.ascontiguousarray(np.asarray(L0_ffn_w_gate, f32)),
        "wu": np.ascontiguousarray(np.asarray(L0_ffn_w_up, f32)),
        "wd": np.ascontiguousarray(np.asarray(L0_ffn_w_down, f32)),
        "wi": np.ascontiguousarray(np.asarray(L0_attn_W_in, f32)),
        "wb": np.ascontiguousarray(np.asarray(L0_attn_W_bc, f32)),
        "wo": np.ascontiguousarray(np.asarray(L0_attn_W_out, f32)),
        "bin": np.ascontiguousarray(np.asarray(L0_attn_b_in, f32).reshape(KB, 128).T),
        "bbc": np.ascontiguousarray(np.asarray(L0_attn_b_bc, f32).reshape(KB, 128).T),
        "bout": np.ascontiguousarray(np.asarray(L0_attn_b_out, f32).reshape(KD, 128).T),
        "cwd": np.ascontiguousarray(np.asarray(L0_attn_conv_w, f32).reshape(KI, 128, 4).transpose(1, 2, 0)),
        "cbd": np.ascontiguousarray(np.asarray(L0_attn_conv_b, f32).reshape(KI, 128).T),
        "ldd": np.ascontiguousarray(np.asarray(L0_attn_log_decay, f32).reshape(KI, 128).T),
    }
    in_maps = []
    for i in range(8):
        xc = np.ascontiguousarray(x[2 * i:2 * i + 2].transpose(0, 2, 1))
        m = dict(com)
        m["x"] = xc
        in_maps.append(m)
    res = run_bass_kernel_spmd(nc, in_maps, core_ids=list(range(8)))
    outs = []
    for i in range(8):
        yc = np.asarray(res.results[i]["y"])
        outs.append(yc.transpose(0, 2, 1))
    return np.ascontiguousarray(np.concatenate(outs, axis=0))
